# version 19
# baseline (speedup 1.0000x reference)
import contextlib
import numpy as np
import ml_dtypes

import concourse.bass as bass
import concourse.mybir as mybir
from concourse.bass_utils import run_bass_kernel_spmd

F32 = mybir.dt.float32
BF16 = mybir.dt.bfloat16
AF = mybir.ActivationFunctionType
ALU = mybir.AluOpType
AX = mybir.AxisListType

SBUF_BASE = 16512

CFG_FULL = dict(d_model=2048, own=1024, part=1024, heads=16, experts=32,
                topk=4, dff=1024)

DETECT_RACES = True


def derive(cfg):
    c = dict(cfg)
    c["NCH"] = cfg["d_model"] // 128
    c["NQ"] = cfg["own"] // 128
    c["NT"] = (cfg["own"] + cfg["part"]) // 128
    c["T"] = cfg["own"] + cfg["part"]
    c["NI"] = cfg["dff"] // 128
    c["ST"] = min(512, cfg["own"])
    c["NST"] = cfg["own"] // c["ST"]
    c["DS"] = min(512, cfg["d_model"])
    c["NDS"] = cfg["d_model"] // c["DS"]
    c["QS"] = 1.0 / float(np.sqrt(np.float32(128)))
    mean_cnt = cfg["own"] * cfg["topk"] // cfg["experts"]
    c["CAP"] = min(384, max(128, ((3 * mean_cnt + 127) // 128) * 128))
    c["NCT"] = c["CAP"] // 128
    c["RW"] = ((cfg["d_model"] + 2 * cfg["experts"] + 2 + 15) // 16) * 16
    return c


class EB:

    ENGS = ("T", "V", "S", "G", "Y")

    def __init__(self, nc, active, env):
        self.nc = nc
        self.active = active
        self.env = env
        self.cnt = {e: 0 for e in self.ENGS}
        self.dmacnt = dict.fromkeys(env["dsems"], 0)
        self.waited = {}

    def _eng(self, e):
        nc = self.nc
        return {"T": nc.tensor, "V": nc.vector, "S": nc.scalar,
                "G": nc.gpsimd, "Y": nc.sync}[e]

    def wait(self, e, tok):
        if tok is None or self.active != e:
            return
        sem, val = tok
        if val <= 0:
            return
        key = id(sem)
        if self.waited.get(key, -1) < val:
            self._eng(e).wait_ge(sem, val)
            self.waited[key] = val

    def op(self, e, fn, waits=(), inc=True):
        if self.active == e:
            for tk in waits:
                self.wait(e, tk)
            if e in ("V", "S", "G") and self.cnt[e] > 0:
                self.wait(e, (self.env["sems"][e], self.cnt[e]))
            ins = fn(self._eng(e))
            if inc:
                ins.then_inc(self.env["sems"][e], 1)
        if inc:
            self.cnt[e] += 1
            return (self.env["sems"][e], self.cnt[e])
        return None

    def tok(self, e):
        return (self.env["sems"][e], self.cnt[e])

    def dma(self, q, sem_name, out, in_, waits=()):
        sem = self.env["dsems"][sem_name]
        self.dmacnt[sem_name] += 16
        if self.active == q:
            for tk in waits:
                self.wait(q, tk)
            self._eng(q).dma_start(out=out, in_=in_).then_inc(sem, 16)
        return (sem, self.dmacnt[sem_name])

    def wm(self):
        return [(self.env["sems"][e], self.cnt[e]) for e in ("T", "V", "S")]


def build_program(cfg):
    c = derive(cfg)
    D, H, E, I = cfg["d_model"], cfg["heads"], cfg["experts"], cfg["dff"]
    NCH, NQ, NT, NI = c["NCH"], c["NQ"], c["NT"], c["NI"]
    ST, NST, DS, NDS = c["ST"], c["NST"], c["DS"], c["NDS"]
    T = c["T"]
    OWN = cfg["own"]
    TOPK = cfg["topk"]
    CAP, NCT, RW = c["CAP"], c["NCT"], c["RW"]
    CAPTOT = E * CAP

    nc = bass.Bass("TRN2", target_bir_lowering=False,
                   detect_race_conditions=DETECT_RACES)

    xo = nc.dram_tensor("x_own", [OWN, D], F32, kind="ExternalInput").ap()
    xp = nc.dram_tensor("x_part", [cfg["part"], D], F32, kind="ExternalInput").ap()
    Wq = nc.dram_tensor("Wq", [D, H * 256], F32, kind="ExternalInput").ap()
    Wk = nc.dram_tensor("Wk", [D, H * 256], F32, kind="ExternalInput").ap()
    Wv = nc.dram_tensor("Wv", [D, H * 128], F32, kind="ExternalInput").ap()
    Wo = nc.dram_tensor("Wo", [H * 128, D], F32, kind="ExternalInput").ap()
    ns_h = nc.dram_tensor("norm_scale", [D], F32, kind="ExternalInput").ap()
    lam_h = nc.dram_tensor("lam", [H], F32, kind="ExternalInput").ap()
    gw_h = nc.dram_tensor("gate_w", [E, D], F32, kind="ExternalInput").ap()
    gb_h = nc.dram_tensor("gate_b", [E], F32, kind="ExternalInput").ap()
    wg_h = nc.dram_tensor("w_gate", [E, D, I], F32, kind="ExternalInput").ap()
    bg_h = nc.dram_tensor("b_gate", [E, I], F32, kind="ExternalInput").ap()
    wu_h = nc.dram_tensor("w_up", [E, D, I], F32, kind="ExternalInput").ap()
    bu_h = nc.dram_tensor("b_up", [E, I], F32, kind="ExternalInput").ap()
    wd_h = nc.dram_tensor("w_down", [E, I, D], F32, kind="ExternalInput").ap()
    bd_h = nc.dram_tensor("b_down", [E, D], F32, kind="ExternalInput").ap()
    y_h = nc.dram_tensor("y_own", [OWN, D], F32, kind="ExternalOutput").ap()

    id_bf_h = nc.inline_tensor(
        np.eye(128, dtype=np.float32).astype(ml_dtypes.bfloat16), name="idbf").ap()
    id_f_h = nc.inline_tensor(np.eye(128, dtype=np.float32), name="idf").ap()
    iota_e_h = nc.inline_tensor(
        np.tile(np.arange(E, dtype=np.float32), (128, 1)), name="iotae").ap()
    tokid_h = nc.inline_tensor(
        (np.arange(NQ, dtype=np.int32)[None, :] * 128
         + np.arange(128, dtype=np.int32)[:, None]).copy(), name="tokid").ap()
    xarena = nc.dram_tensor("xarena", [CAPTOT, RW], BF16, kind="Internal").ap()
    yarena = nc.dram_tensor("yarena", [OWN, D], BF16, kind="Internal").ap()

    def at(name, shape, dt, off):
        return nc.alloc_sbuf_tensor_at(
            name, shape, dt, offset=SBUF_BASE + off).ap()

    KB = 1024
    A32 = lambda x: (x + 31) & ~31
    cur = [0]

    def nxt(nbytes):
        off = A32(cur[0])
        cur[0] = off + nbytes
        return off

    ident_bf = at("ident_bf", [128, 128], BF16, nxt(256))
    ident_f = at("ident_f", [128, 128], F32, nxt(512))
    ones_col = at("ones_col", [1, 128], F32, nxt(512))
    ones_row = at("ones_row", [1, 512], F32, nxt(2048))
    lam_b = at("lam_b", [128, H], F32, nxt(H * 4))
    stats = at("stats", [128, 64], F32, nxt(256))
    gwT = at("gwT", [128, NCH * 32], BF16, nxt(NCH * 64))
    gb_row = at("gb_row", [1, E], F32, nxt(E * 4))
    bgT_all = at("bgT_all", [128, NI * E], F32, nxt(NI * E * 4))
    buT_all = at("buT_all", [128, NI * E], F32, nxt(NI * E * 4))
    bd_all = at("bd_all", [E, D], F32, nxt(D * 4))
    ns_row = at("ns_row", [1, D], F32, nxt(D * 4))
    iota_e = at("iota_e", [128, E], F32, nxt(E * 4))
    tokid_t = at("tokid_t", [128, NQ], mybir.dt.int32, nxt(NQ * 4))
    assert cur[0] <= 24 * KB, cur[0]

    R1, R2, R3, R4, R5 = 24 * KB, 88 * KB, 120 * KB, 160 * KB, 200 * KB
    ln_xT = at("ln_xT", [128, NCH * T], BF16, R1)
    resid = at("resid", [128, NQ * D], F32, R1)
    assert max(NCH * T * 2, NQ * D * 4) <= R2 - R1
    stage_f = at("stage_f", [E, max(I, D)], F32, R2)
    gw_sb = at("gw_sb", [E, D], BF16, R2 + max(I, D) * 4)
    out_sb = at("out_sb", [128, NQ * D], BF16, R2)
    ln2_xT = at("ln2_xT", [128, NCH * OWN], BF16, R2)
    assert NQ * D * 2 <= R3 - R2 and NCH * OWN * 2 <= R3 - R2
    e1 = at("e1", [128, T], F32, R3)
    e2 = at("e2", [128, T], F32, R3 + T * 4)
    o = R3 + 2 * T * 4
    k1T = at("k1T", [128, T], BF16, o); o += T * 2
    k2T = at("k2T", [128, T], BF16, o); o += T * 2
    v_sb = at("v_sb", [128, T], BF16, o); o += T * 2
    q1T = at("q1T", [128, OWN], BF16, o); o += OWN * 2
    q2T = at("q2T", [128, OWN], BF16, o); o += OWN * 2
    p_sb = at("p_sb", [128, T], BF16, o); o += T * 2
    pt_sb = at("pt_sb", [128, T], BF16, o); o += T * 2
    assert o <= R4, o
    out_nT = at("out_nT", [128, NCH * OWN], BF16, R3)
    ns_b = at("ns_b", [128, D], F32, R3 + NCH * OWN * 2)
    assert R3 + NCH * OWN * 2 + D * 4 <= R4
    actT = at("actT", [128, NI * CAP], BF16, R3)
    dof = R3 + NI * CAP * 2
    def nx2(n):
        nonlocal_holder[0] = (nonlocal_holder[0] + 31) & ~31
        off = nonlocal_holder[0]
        nonlocal_holder[0] = off + n
        return off
    nonlocal_holder = [dof]
    xtT = at("xtT", [128, NCH * CAP], BF16, nx2(NCH * CAP * 2))
    w_sb = at("w_sb", [128, NQ * E], F32, nx2(NQ * E * 4))
    lnscr_off = nx2(D * 2)
    lnscr = at("lnscr", [128, D], BF16, lnscr_off)
    wT_sb = at("wT_sb", [E, OWN], F32, nx2(OWN * 4))
    eL_sb = at("eL_sb", [128, E], F32, nx2(E * 4))
    gcl = at("gcl", [128, CAP], F32, nx2(CAP * 4))
    ucl = at("ucl", [128, CAP], F32, nx2(CAP * 4))
    sig = at("sig", [128, CAP], F32, nx2(CAP * 4))
    idx_sb = at("idx_sb", [128, NQ * TOPK], mybir.dt.int32, nx2(NQ * TOPK * 4))
    topidx = at("topidx", [128, NQ * TOPK], F32, nx2(NQ * TOPK * 4))
    idx8 = at("idx8", [128, 8], mybir.dt.uint32, nx2(32))
    wbuf = at("wbuf", [128, 4], F32, nx2(16))
    tibuf = at("tibuf", [128, 4], mybir.dt.int32, nx2(16))
    ysb_r3 = [at(f"ysbr{i}", [128, D], BF16, nx2(D * 2))
              for i in range(max(0, NCT - 2))]
    dof = nonlocal_holder[0]
    assert dof <= R4, dof
    xt_sb = at("xt_sb", [128, NCT, RW], BF16, R2)
    junk_d = at("junk_d", [128, D], BF16, R3)
    ysbt = []
    for i in range(min(2, NCT)):
        ysbt.append(at(f"ysb{i}", [128, D], BF16,
                       R2 + NCT * RW * 2 + i * D * 2))
    assert NCT * RW * 2 + min(2, NCT) * D * 2 <= R3 - R2
    ysbt.extend(ysb_r3)
    posT = at("posT", [E, OWN], F32, R5)
    maskT = at("maskT", [E, OWN], F32, R4 + 36 * KB)
    assert R5 + OWN * 4 <= 212832
    zt = at("zt", [128, RW], BF16, R3)
    lnscr2 = at("lnscr2", [128, RW], BF16, R4)
    xsl = [at(f"xsl{i}", [128, D], F32, R4 + i * D * 4) for i in range(2)]
    WQKB = NCH * 1024 + NCH * 256
    wqkv = []
    for sl in range(2):
        base = R4 + sl * WQKB
        wqkv.append((at(f"wq{sl}", [128, NCH, 256], BF16, base),
                     at(f"wk{sl}", [128, NCH, 256], BF16, base + NCH * 512),
                     at(f"wv{sl}", [128, NCH, 128], BF16, base + NCH * 1024)))
    assert 2 * WQKB <= 40 * KB
    wosl = [at(f"wo{sl}", [128, NCH, DS], BF16, R4 + sl * NCH * DS * 2)
            for sl in range(2)]
    assert 2 * NCH * DS * 2 <= 40 * KB
    gusl = [(at(f"wgp{sl}", [128, NCH, 128], BF16, R4 + sl * NCH * 512),
             at(f"wup{sl}", [128, NCH, 128], BF16,
                R4 + sl * NCH * 512 + NCH * 256)) for sl in range(3)]
    wustg = [at(f"wust{i}", [128, NCH // 2, 128], F32,
                R2 + NCT * RW * 2 + min(2, NCT) * D * 2 + i * NCH * 256)
             for i in range(2)]
    wustg += [at(f"wust{i + 2}", [128, NCH // 2, 128], F32,
                 lnscr_off + i * NCH * 256) for i in range(2)]
    assert NCT * RW * 2 + min(2, NCT) * D * 2 + 2 * NCH * 256 <= R3 - R2
    wdsl = [at(f"wdp{sl}", [128, NI, DS], BF16,
               R4 + 3 * NCH * 512 + sl * NI * DS * 2) for sl in range(2)]
    assert 3 * NCH * 512 + 2 * NI * DS * 2 <= 40 * KB
    junk_bf = at("junk_bf", [128, D], BF16, R4 + 32 * KB)
    tmp_f = at("tmp_f", [128, D], F32, R4 + 36 * KB)
    xsl2 = [at(f"xc{i}", [128, DS], F32, R5 + 2 * KB + i * DS * 4)
            for i in range(2)]
    assert R5 + 2 * KB + 2 * DS * 4 <= 212832

    es_glob = contextlib.ExitStack()
    psA = es_glob.enter_context(nc.psum_tensor("psA", [128, 2048], F32))
    psB = es_glob.enter_context(nc.psum_tensor("psB", [128, 2048], F32))

    sems = {e: es_glob.enter_context(nc.semaphore(f"sem{e}")) for e in EB.ENGS}
    dnames = (["cstY", "cstG", "st", "pfx", "pfy", "sca", "xld", "ya0", "ya1", "yrd"]
              + [f"xsl{i}" for i in range(2)]
              + [f"wqk{i}" for i in range(2)]
              + [f"wo{i}" for i in range(2)]
              + [f"gu{i}" for i in range(3)]
              + [f"wu{i}" for i in range(4)]
              + [f"wd{i}" for i in range(2)]
              + [f"xc{i}" for i in range(2)])
    dsems = {n: es_glob.enter_context(nc.semaphore(f"d_{n}")) for n in dnames}
    env = {"sems": sems, "dsems": dsems}

    EPS_LN = 1e-5
    EPS_RMS = 1e-6

    def psum_bf(ps, col0_f32, ncols_bf):
        return ps[:, col0_f32:col0_f32 + (ncols_bf + 1) // 2].bitcast(BF16)

    def chunks(apT, n_per, width):
        return apT[:].rearrange("p (c t) -> p c t", t=n_per)

    def walk(b: EB):
        bank_tok = {}

        def mmgroup(ps, col0, n, lhsTs, rhss, waits=()):
            tk = None
            nmm = len(lhsTs)
            for i, (lh, rh) in enumerate(zip(lhsTs, rhss)):
                w = []
                if i == 0:
                    w = list(waits)
                    for cb in range(col0 // 512, (col0 + n + 511) // 512):
                        w.append(bank_tok.get((id(ps), cb)))
                tk = b.op("T", lambda t_, lh=lh, rh=rh, i=i: t_.matmul(
                    ps[:, col0:col0 + n], lh, rh,
                    start=(i == 0), stop=(i == nmm - 1)),
                    waits=w, inc=(i == nmm - 1))
            return tk

        def bank_free(ps, col0, n, tok):
            for cb in range(col0 // 512, (col0 + n + 511) // 512):
                bank_tok[(id(ps), cb)] = tok

        b.dma("Y", "cstY", ident_bf[:], id_bf_h[:])
        b.dma("Y", "cstY", ident_f[:], id_f_h[:])
        b.dma("Y", "cstY", lam_b[0:1, :], lam_h[None, :])
        b.dma("Y", "cstY", ns_row[:], ns_h[None, :])
        b.dma("Y", "cstY", gb_row[:], gb_h[None, :])
        b.dma("Y", "cstY", bd_all[:], bd_h[:])
        t_gw = b.dma("G", "cstG", gw_sb[:], gw_h[:])
        b.dma("Y", "cstY", iota_e[:], iota_e_h[:])
        b.dma("Y", "cstY", tokid_t[:], tokid_h[:])
        t_bg = b.dma("Y", "cstY", stage_f[0:E, 0:I], bg_h[:])
        tok_cy = t_bg
        t_id1 = t_id2 = t_lam = tok_cy
        t_m1 = b.op("V", lambda v: v.memset(ones_col[:], 1.0))
        b.op("V", lambda v: v.memset(ones_row[:], 1.0))
        eps_ln = stats[:, 21:22]
        eps_rms = stats[:, 22:23]
        b.op("V", lambda v: v.memset(eps_ln, EPS_LN))
        b.op("V", lambda v: v.memset(eps_rms, EPS_RMS))
        b.op("V", lambda v: v.memset(zt[:], 0.0))
        t_zt = b.op("V", lambda v: v.memset(
            zt[:, D + 2 * E:D + 2 * E + 2].bitcast(mybir.dt.int32), OWN))
        tk = b.op("T", lambda t_: t_.matmul(psA[:, 0:H], ones_col[:],
                                            lam_b[0:1, :], start=True, stop=True),
                  waits=[t_lam, t_m1])
        t_lb = b.op("V", lambda v: v.tensor_copy(lam_b[:], psA[:, 0:H]),
                    waits=[tk])
        one_minus = stats[:, 20:21]
        b.op("V", lambda v: v.reduce_sum(one_minus, lam_b[:], axis=AX.X),
             waits=[t_lb])
        b.op("V", lambda v: v.tensor_scalar(
            out=one_minus, in0=one_minus, scalar1=-1.0 / H, scalar2=1.0,
            op0=ALU.mult, op1=ALU.add))
        gw_ev = []
        for cc in range(NCH):
            tt = b.op("T", lambda t_, cc=cc: t_.transpose(
                psum_bf(psB, (cc % 4) * 512, 2 * E)[:, 0:E],
                gw_sb[0:E, cc * 128:(cc + 1) * 128], ident_bf[0:E, 0:E]),
                waits=[t_gw, t_id1] + ([gw_ev[cc - 4]] if cc >= 4 else []))
            gw_ev.append(b.op("V", lambda v, cc=cc, tt=tt: v.tensor_copy(
                gwT[:, cc * 32:cc * 32 + E],
                psum_bf(psB, (cc % 4) * 512, 2 * E)[:, 0:E]), waits=[tt]))

        def bias_table(dst, t_loaded):
            toks = []
            for i in range(NI):
                col = 512 + (i % 2) * 512
                tt = b.op("T", lambda t_, i=i, col=col: t_.transpose(
                    psA[0:128, col:col + E],
                    stage_f[0:E, i * 128:(i + 1) * 128], ident_f[0:E, 0:E]),
                    waits=[t_loaded, t_id2] + ([toks[i - 2]] if i >= 2 else []))
                toks.append(b.op("V", lambda v, i=i, col=col, tt=tt: v.tensor_copy(
                    dst[:, i * E:(i + 1) * E], psA[:, col:col + E]), waits=[tt]))
            return toks
        tg_tab = bias_table(bgT_all, t_bg)
        t_bu = b.dma("Y", "cstY", stage_f[0:E, 0:I], bu_h[:],
                     waits=[tg_tab[-1]])
        tu_tab = bias_table(buT_all, t_bu)
        init_v = b.tok("V")
        bank_free(psA, 0, 2048, init_v)
        bank_free(psB, 0, 2048, init_v)

        negm = stats[:, 0:1]
        sq = stats[:, 1:2]
        var = stats[:, 2:3]
        m2c = stats[:, 3:4]
        rstd = stats[:, 4:5]
        t_xread = {}
        a_ev = init_v
        for tt in range(NT):
            src = xo if tt < NQ else xp
            row0 = (tt if tt < NQ else tt - NQ) * 128
            sl = tt % 2
            t_x = b.dma("Y", f"xsl{sl}", xsl[sl][:], src[row0:row0 + 128, :],
                        waits=[t_xread.get(tt - 2)])
            b.op("V", lambda v, sl=sl: v.reduce_sum(
                negm, xsl[sl][:], axis=AX.X, negate=True), waits=[t_x])
            b.op("V", lambda v: v.tensor_scalar_mul(negm, negm, 1.0 / D))
            t_sq = b.op("S", lambda s, sl=sl: s.activation(
                junk_bf[:], xsl[sl][:], AF.Square, accum_out=sq), waits=[t_x])
            b.op("V", lambda v: v.tensor_mul(m2c, negm, negm))
            b.op("V", lambda v: v.tensor_scalar(
                out=var, in0=sq, scalar1=1.0 / D, scalar2=None, op0=ALU.mult),
                waits=[t_sq])
            t_v3 = b.op("V", lambda v: v.tensor_sub(var, var, m2c))
            b.op("S", lambda s: s.activation(rstd, var, AF.Ln, bias=eps_ln),
                 waits=[t_v3])
            t_r2 = b.op("S", lambda s: s.activation(rstd, rstd, AF.Exp,
                                                    scale=-0.5))
            t_ln = b.op("V", lambda v, sl=sl: v.tensor_scalar(
                out=lnscr[:], in0=xsl[sl][:], scalar1=negm, scalar2=rstd,
                op0=ALU.add, op1=ALU.mult), waits=[t_r2])
            t_xread[tt] = t_ln
            ttr = None
            for cc in range(NCH):
                ttr = b.op("T", lambda t_, cc=cc: t_.transpose(
                    psum_bf(psA, 0, 2048)[:, cc * 128:(cc + 1) * 128],
                    lnscr[:, cc * 128:(cc + 1) * 128], ident_bf[:]),
                    waits=([t_ln, a_ev] if cc == 0 else []),
                    inc=(cc == NCH - 1))
            lnv = ln_xT[:].rearrange("p (c t) -> p c t", t=T)
            pv = psum_bf(psA, 0, 2048)[:].rearrange("p (c t) -> p c t", t=128)
            a_ev = b.op("V", lambda v, tt=tt, lnv=lnv, pv=pv: v.tensor_copy(
                lnv[:, :, tt * 128:(tt + 1) * 128], pv[:, 0:NCH, :]),
                waits=[ttr])
        bank_free(psA, 0, 1024, a_ev)
        wmA = b.wm()
        xav = xarena.rearrange("(n p) c -> n p c", p=128)
        t_pfx = None
        for blk in range(CAPTOT // 128):
            t_pfx = b.dma("Y", "pfx", xav[blk], zt[:], waits=[t_zt])
        yav = yarena.rearrange("(n p) c -> n p c", p=128)
        t_pfy = None
        for blk in range(OWN // 128):
            t_pfy = b.dma("Y", "pfy", yav[blk], zt[:, 0:D], waits=[t_zt])

        nm1 = stats[:, 5:6]
        nm2 = stats[:, 6:7]
        z1 = stats[:, 7:8]
        z2 = stats[:, 8:9]
        zd = stats[:, 9:10]
        r1s = stats[:, 10:11]
        b2s = stats[:, 11:12]
        rzd = stats[:, 12:13]
        lnv = ln_xT[:].rearrange("p (c t) -> p c t", t=T)
        t_wread = {}
        head_end = None
        for h in range(H):
            sl = h % 2
            wqs, wks, wvs = wqkv[sl]
            wq_src = Wq.rearrange("(c p) n -> p c n", p=128)[:, :, h * 256:(h + 1) * 256]
            wk_src = Wk.rearrange("(c p) n -> p c n", p=128)[:, :, h * 256:(h + 1) * 256]
            wv_src = Wv.rearrange("(c p) n -> p c n", p=128)[:, :, h * 128:(h + 1) * 128]
            war = [t_wread.get(h - 2)] + (wmA if h < 2 else [])
            half = max(1, NCH // 2)
            tw = None
            for piece in range(0, NCH, half):
                pe = min(piece + half, NCH)
                b.dma("G", f"wqk{sl}", wqs[:, piece:pe, :],
                      wq_src[:, piece:pe, :], waits=war)
                war = []
                b.dma("G", f"wqk{sl}", wks[:, piece:pe, :],
                      wk_src[:, piece:pe, :])
                tw = b.dma("G", f"wqk{sl}", wvs[:, piece:pe, :],
                           wv_src[:, piece:pe, :])
            nkb = max(1, T // 512)
            kb = T // nkb
            for comp, kT in enumerate((k1T, k2T)):
                wofs = comp * 128
                ps = psA if comp == 0 else psB
                for blk in range(nkb):
                    col = blk * kb
                    pc = (blk % 4) * 512
                    tk = mmgroup(ps, pc, kb,
                                 [wks[:, cc, wofs:wofs + 128] for cc in range(NCH)],
                                 [lnv[:, cc, col:col + kb] for cc in range(NCH)],
                                 waits=[tw, head_end] + wmA)
                    te = b.op("S", lambda s, kT=kT, col=col, ps=ps, pc=pc, kb=kb:
                              s.activation(kT[:, col:col + kb],
                                           ps[:, pc:pc + kb], AF.Copy),
                              waits=[tk])
                    bank_free(ps, pc, kb, te)
            nqb = max(1, OWN // 512)
            qb = OWN // nqb
            for comp, qT in enumerate((q1T, q2T)):
                wofs = comp * 128
                for blk in range(nqb):
                    col = blk * qb
                    pc = ((comp * nqb + blk) % 4) * 512
                    tk = mmgroup(psA, pc, qb,
                                 [wqs[:, cc, wofs:wofs + 128] for cc in range(NCH)],
                                 [lnv[:, cc, col:col + qb] for cc in range(NCH)],
                                 waits=[tw])
                    te = b.op("S", lambda s, qT=qT, col=col, pc=pc, qb=qb:
                              s.activation(qT[:, col:col + qb],
                                           psA[:, pc:pc + qb], AF.Copy,
                                           scale=c["QS"]), waits=[tk])
                    bank_free(psA, pc, qb, te)
            for ttile in range(NT):
                pc = (ttile % 4) * 512
                tk = mmgroup(psB, pc, 128,
                             [lnv[:, cc, ttile * 128:(ttile + 1) * 128]
                              for cc in range(NCH)],
                             [wvs[:, cc, :] for cc in range(NCH)],
                             waits=[tw])
                te = b.op("S", lambda s, ttile=ttile, pc=pc:
                          s.activation(v_sb[:, ttile * 128:(ttile + 1) * 128],
                                       psB[:, pc:pc + 128], AF.Copy),
                          waits=[tk])
                bank_free(psB, pc, 128, te)
            t_wread[h] = b.tok("T")
            nsb = max(1, T // 512)
            sbw = T // nsb
            pend = None
            last_diff = None
            last_pv = None
            last_ptr = None
            for q in range(NQ + 1):
                if q < NQ:
                    t_s1 = None
                    for blk in range(nsb):
                        w0 = ([bank_tok.get((id(psA), cb)) for cb in range(4)]
                              if blk == 0 else [])
                        t_s1 = b.op("T", lambda t_, blk=blk, q=q:
                                    t_.matmul(psA[:, blk * sbw:(blk + 1) * sbw],
                                              q1T[:, q * 128:(q + 1) * 128],
                                              k1T[:, blk * sbw:(blk + 1) * sbw],
                                              start=True, stop=True),
                                    waits=w0, inc=(blk == nsb - 1))
                    t_e1 = b.op("S", lambda s: s.activation(
                        e1[:], psA[:, 0:T], AF.Exp, accum_out=z1),
                        waits=[t_s1, last_diff]
                        + ([t_pfx, t_pfy] if h == 0 and q == 0 else []))
                    bank_free(psA, 0, 2048, t_e1)
                if q < NQ:
                    t_s2 = None
                    for blk in range(nsb):
                        w0 = ([bank_tok.get((id(psA), cb)) for cb in range(4)]
                              if blk == 0 else [])
                        t_s2 = b.op("T", lambda t_, blk=blk, q=q:
                                    t_.matmul(psA[:, blk * sbw:(blk + 1) * sbw],
                                              q2T[:, q * 128:(q + 1) * 128],
                                              k2T[:, blk * sbw:(blk + 1) * sbw],
                                              start=True, stop=True),
                                    waits=w0, inc=(blk == nsb - 1))
                if pend is not None:
                    qp, dd = pend
                    ptv = psum_bf(psB, 0, 2048)
                    ttr = None
                    for tau in range(NT):
                        ttr = b.op("T", lambda t_, tau=tau, ptv=ptv: t_.transpose(
                            ptv[:, tau * 128:(tau + 1) * 128],
                            p_sb[:, tau * 128:(tau + 1) * 128], ident_bf[:]),
                            waits=([dd["t_pd"], bank_tok.get((id(psB), 0)),
                                    bank_tok.get((id(psB), 1))]
                                   if tau == 0 else []),
                            inc=(tau == NT - 1))
                    last_ptr = ttr
                    t_ptev = b.op("V", lambda v, ptv=ptv: v.tensor_copy(
                        pt_sb[:, 0:T], ptv[:, 0:T]), waits=[ttr, last_pv])
                    bank_free(psB, 0, 1024, t_ptev)
                    pvslot = 1024 + (qp % 2) * 512
                    t_pv = None
                    for tau in range(NT):
                        t_pv = b.op("T", lambda t_, tau=tau, pvslot=pvslot:
                                    t_.matmul(
                                        psB[:, pvslot:pvslot + 128],
                                        pt_sb[:, tau * 128:(tau + 1) * 128],
                                        v_sb[:, tau * 128:(tau + 1) * 128],
                                        start=(tau == 0), stop=(tau == NT - 1)),
                                    waits=([t_ptev,
                                            bank_tok.get((id(psB),
                                                          pvslot // 512))]
                                           if tau == 0 else []),
                                    inc=(tau == NT - 1))
                    last_pv = t_pv
                    b.op("V", lambda v: v.reciprocal(rzd, zd),
                         waits=[dd["t_pd"]])
                    t_pvev = b.op("V", lambda v, qp=qp, pvslot=pvslot, h=h:
                                  v.tensor_scalar(
                                      out=out_sb[:, qp * D + h * 128:
                                                 qp * D + (h + 1) * 128],
                                      in0=psB[:, pvslot:pvslot + 128],
                                      scalar1=rzd, scalar2=None,
                                      op0=ALU.mult), waits=[t_pv])
                    bank_free(psB, pvslot, 512, t_pvev)
                    head_end = t_pvev
                    pend = None
                if q == NQ:
                    break
                t_e2 = b.op("S", lambda s: s.activation(
                    e2[:], psA[:, 0:T], AF.Exp, accum_out=z2),
                    waits=[t_s2])
                bank_free(psA, 0, 2048, t_e2)
                b.op("V", lambda v: v.reciprocal(r1s, z1), waits=[t_e1])
                b.op("V", lambda v: v.reciprocal(b2s, z2), waits=[t_e2])
                b.op("V", lambda v, h=h: v.tensor_mul(b2s, b2s,
                                                      lam_b[:, h:h + 1]))
                b.op("V", lambda v: v.tensor_scalar_mul(e2[:], e2[:], b2s))
                t_diff = b.op("V", lambda v: v.scalar_tensor_tensor(
                    out=e2[:], in0=e1[:], scalar=r1s, in1=e2[:],
                    op0=ALU.mult, op1=ALU.subtract))
                last_diff = t_diff
                t_pd = b.op("S", lambda s: s.activation(
                    p_sb[:, 0:T], e2[:], AF.Exp, accum_out=zd),
                    waits=[t_diff, last_ptr])
                pend = (q, {"t_pd": t_pd})
        wmB = b.wm()

        tnb = None
        for j in range(NDS):
            tk = b.op("T", lambda t_, j=j: t_.matmul(
                psA[:, (j % 4) * 512:(j % 4) * 512 + DS], ones_col[:],
                ns_row[0:1, j * DS:(j + 1) * DS], start=True, stop=True),
                waits=wmB + [bank_tok.get((id(psA), j % 4))])
            tnb = b.op("V", lambda v, j=j: v.tensor_copy(
                ns_b[:, j * DS:(j + 1) * DS],
                psA[:, (j % 4) * 512:(j % 4) * 512 + DS]), waits=[tk])
            bank_free(psA, (j % 4) * 512, DS, tnb)
        sumsq = stats[:, 14:15]
        rstd2 = stats[:, 15:16]
        c_ev = None
        t_t1 = None
        for q in range(NQ):
            b.op("S", lambda s, q=q: s.activation(
                junk_bf[:], out_sb[:, q * D:(q + 1) * D], AF.Square,
                accum_out=sumsq), waits=wmB + [t_t1])
            b.op("S", lambda s: s.activation(
                rstd2, sumsq, AF.Ln, scale=1.0 / D, bias=eps_rms))
            t_b_ = b.op("S", lambda s: s.activation(rstd2, rstd2, AF.Exp,
                                                    scale=-0.5))
            b.op("V", lambda v: v.tensor_mul(rstd2, rstd2, one_minus),
                 waits=[t_b_])
            t_t1 = b.op("V", lambda v, q=q: v.tensor_scalar(
                out=tmp_f[:], in0=out_sb[:, q * D:(q + 1) * D],
                scalar1=rstd2, scalar2=None, op0=ALU.mult))
            t_on = b.op("V", lambda v, q=q: v.tensor_mul(
                out_sb[:, q * D:(q + 1) * D], tmp_f[:], ns_b[:]),
                waits=[tnb])
            ttr = None
            for cc in range(NCH):
                ttr = b.op("T", lambda t_, cc=cc, q=q: t_.transpose(
                    psum_bf(psB, 0, 2048)[:, cc * 128:(cc + 1) * 128],
                    out_sb[:, q * D + cc * 128: q * D + (cc + 1) * 128],
                    ident_bf[:]),
                    waits=([t_on, c_ev, bank_tok.get((id(psB), 0)),
                            bank_tok.get((id(psB), 1))] if cc == 0 else []),
                    inc=(cc == NCH - 1))
            onv = out_nT[:].rearrange("p (c t) -> p c t", t=OWN)
            pv = psum_bf(psB, 0, 2048)[:].rearrange("p (c t) -> p c t", t=128)
            c_ev = b.op("V", lambda v, q=q, onv=onv, pv=pv: v.tensor_copy(
                onv[:, :, q * 128:(q + 1) * 128], pv[:, 0:NCH, :]),
                waits=[ttr])
        bank_free(psB, 0, 1024, c_ev)
        wmC1 = b.wm()
        onv = out_nT[:].rearrange("p (c t) -> p c t", t=OWN)
        t_word = {}
        for dsl in range(NDS):
            sl = dsl % 2
            wo_src = Wo.rearrange("(c p) n -> p c n", p=128)[:, :, dsl * DS:(dsl + 1) * DS]
            half = max(1, NCH // 2)
            war = [t_word.get(dsl - 2)] + (wmC1 if dsl < 2 else [])
            twd = None
            for piece in range(0, NCH, half):
                pe = min(piece + half, NCH)
                twd = b.dma("G", f"wo{sl}", wosl[sl][:, piece:pe, :],
                            wo_src[:, piece:pe, :], waits=war)
                war = []
            for q in range(NQ):
                n_lin = dsl * NQ + q
                xslq = xsl2[n_lin % 2]
                t_x = b.dma("Y", f"xc{n_lin % 2}", xslq[:],
                            xo[q * 128:(q + 1) * 128, dsl * DS:(dsl + 1) * DS],
                            waits=[t_xread.get(("C", n_lin - 2))]
                            + (wmC1 if n_lin < 2 else []))
                pc = (q % 4) * 512
                tk = mmgroup(psA, pc, DS,
                             [onv[:, cc, q * 128:(q + 1) * 128]
                              for cc in range(NCH)],
                             [wosl[sl][:, cc, :] for cc in range(NCH)],
                             waits=[twd] + (wmC1 if n_lin == 0 else []))
                t_add = b.op("V", lambda v, q=q, dsl=dsl, pc=pc, xslq=xslq:
                             v.scalar_tensor_tensor(
                                 out=resid[:, q * D + dsl * DS:
                                           q * D + (dsl + 1) * DS],
                                 in0=psA[:, pc:pc + DS], scalar=1.0,
                                 in1=xslq[:], op0=ALU.mult, op1=ALU.add),
                             waits=[tk, t_x])
                bank_free(psA, pc, DS, t_add)
                t_xread[("C", n_lin)] = t_add
            t_word[dsl] = b.tok("T")
        wmC = b.wm()

        d_tr_ev = None
        for q in range(NQ):
            xs = resid[:, q * D:(q + 1) * D]
            b.op("V", lambda v, xs=xs: v.reduce_sum(
                negm, xs, axis=AX.X, negate=True), waits=wmC)
            b.op("V", lambda v: v.tensor_scalar_mul(negm, negm, 1.0 / D))
            t_sq = b.op("S", lambda s, xs=xs: s.activation(
                junk_d[:], xs, AF.Square, accum_out=sq), waits=wmC)
            b.op("V", lambda v: v.tensor_mul(m2c, negm, negm))
            b.op("V", lambda v: v.tensor_scalar(
                out=var, in0=sq, scalar1=1.0 / D, scalar2=None, op0=ALU.mult),
                waits=[t_sq])
            t_v3 = b.op("V", lambda v: v.tensor_sub(var, var, m2c))
            b.op("S", lambda s: s.activation(rstd, var, AF.Ln, bias=eps_ln),
                 waits=[t_v3])
            t_r2 = b.op("S", lambda s: s.activation(rstd, rstd, AF.Exp,
                                                    scale=-0.5))
            t_ln = b.op("V", lambda v, xs=xs: v.tensor_scalar(
                out=lnscr[:], in0=xs, scalar1=negm, scalar2=rstd,
                op0=ALU.add, op1=ALU.mult), waits=[t_r2])
            ttr = None
            for cc in range(NCH):
                ttr = b.op("T", lambda t_, cc=cc: t_.transpose(
                    psum_bf(psA, 0, 2048)[:, cc * 128:(cc + 1) * 128],
                    lnscr[:, cc * 128:(cc + 1) * 128], ident_bf[:]),
                    waits=([t_ln, d_tr_ev, bank_tok.get((id(psA), 0)),
                            bank_tok.get((id(psA), 1))] if cc == 0 else []),
                    inc=(cc == NCH - 1))
            l2v = ln2_xT[:].rearrange("p (c t) -> p c t", t=OWN)
            pv = psum_bf(psA, 0, 2048)[:].rearrange("p (c t) -> p c t", t=128)
            d_tr_ev = b.op("V", lambda v, q=q, l2v=l2v, pv=pv: v.tensor_copy(
                l2v[:, :, q * 128:(q + 1) * 128], pv[:, 0:NCH, :]),
                waits=[ttr])
        bank_free(psA, 0, 1024, d_tr_ev)
        l2v = ln2_xT[:].rearrange("p (c t) -> p c t", t=OWN)
        nm1g = stats[:, 16:17]
        t4s = stats[:, 17:18]
        z4 = stats[:, 18:19]
        rz4 = stats[:, 19:20]
        top8 = stats[:, 24:32]
        for q in range(NQ):
            tk = mmgroup(psA, 0, E,
                         [l2v[:, cc, q * 128:(q + 1) * 128]
                          for cc in range(NCH)] + [ones_col[:]],
                         [gwT[:, cc * 32:cc * 32 + E] for cc in range(NCH)]
                         + [gb_row[:]],
                         waits=[d_tr_ev])
            b.op("V", lambda v: v.max(out=top8, in_=psA[:, 0:E]), waits=[tk])
            b.op("V", lambda v: v.max_index(idx8[:], top8[:], psA[:, 0:E]))
            b.op("V", lambda v, q=q: v.tensor_copy(
                topidx[:, q * TOPK:(q + 1) * TOPK], idx8[:, 0:TOPK]))
            t_n1 = b.op("V", lambda v: v.tensor_scalar_mul(
                nm1g, top8[:, 0:1], -1.0))
            t_t4 = b.op("S", lambda s: s.activation(
                t4s, top8[:, TOPK - 1:TOPK], AF.Exp, bias=nm1g), waits=[t_n1])
            t_eL = b.op("S", lambda s: s.activation(
                eL_sb[:, 0:E], psA[:, 0:E], AF.Exp, bias=nm1g))
            b.op("V", lambda v, q=q: v.scalar_tensor_tensor(
                out=w_sb[:, q * E:(q + 1) * E], in0=eL_sb[:, 0:E], scalar=t4s,
                in1=eL_sb[:, 0:E], op0=ALU.is_ge, op1=ALU.mult,
                accum_out=z4), waits=[t_t4, t_eL])
            b.op("V", lambda v: v.reciprocal(rz4, z4))
            t_ws = b.op("V", lambda v, q=q: v.tensor_scalar_mul(
                w_sb[:, q * E:(q + 1) * E], w_sb[:, q * E:(q + 1) * E], rz4))
            bank_free(psA, 0, 512, t_eL)
            t_wt = b.op("T", lambda t_, q=q: t_.transpose(
                psA[0:E, 512:512 + 128], w_sb[:, q * E:(q + 1) * E],
                ident_f[:]), waits=[t_ws, bank_tok.get((id(psA), 1))])
            t_wtv = b.op("V", lambda v, q=q: v.tensor_copy(
                wT_sb[0:E, q * 128:(q + 1) * 128], psA[0:E, 512:512 + 128]),
                waits=[t_wt])
            bank_free(psA, 512, 512, t_wtv)
            for j in range(NDS):
                pc = 1024 + (j % 2) * 512
                tk2 = b.op("T", lambda t_, q=q, j=j, pc=pc: t_.matmul(
                    psA[:, pc:pc + DS], wT_sb[0:E, q * 128:(q + 1) * 128],
                    bd_all[0:E, j * DS:(j + 1) * DS], start=True, stop=True),
                    waits=[t_wtv, bank_tok.get((id(psA), pc // 512))])
                ta = b.op("V", lambda v, q=q, j=j, pc=pc: v.tensor_add(
                    resid[:, q * D + j * DS: q * D + (j + 1) * DS],
                    resid[:, q * D + j * DS: q * D + (j + 1) * DS],
                    psA[:, pc:pc + DS]), waits=[tk2])
                bank_free(psA, pc, DS, ta)
        t_gate_done = b.tok("V")

        if b.active == "G":
            bc_x_reg = b.nc.gpsimd.alloc_register("bc_x")
            bc_y_reg = b.nc.gpsimd.alloc_register("bc_y")
            b.nc.gpsimd.reg_mov(bc_x_reg, CAPTOT - 1)
            b.nc.gpsimd.reg_mov(bc_y_reg, OWN - 1)
        else:
            bc_x_reg = bc_y_reg = None

        sl_t = stats[:, 33:34]
        g_t = stats[:, 34:35]
        ovf_t = stats[:, 35:36]
        t_mk = b.op("V", lambda v: v.tensor_scalar(
            out=maskT[:], in0=wT_sb[0:E, 0:OWN], scalar1=0.0, scalar2=None,
            op0=ALU.is_gt), waits=[t_gate_done])
        t_sc = b.op("V", lambda v: v.tensor_tensor_scan(
            out=posT[:], data0=maskT[:], data1=maskT[:], initial=0.0,
            op0=ALU.add, op1=ALU.bypass))
        t_idx_done = None
        for tt in range(NQ):
            t_tr = b.op("T", lambda t_, tt=tt: t_.transpose(
                psA[0:128, 0:E], posT[0:E, tt * 128:(tt + 1) * 128],
                ident_f[0:E, 0:E]),
                waits=[t_sc, bank_tok.get((id(psA), 0))])
            t_ptk = b.op("V", lambda v: v.tensor_copy(eL_sb[:], psA[:, 0:E]),
                         waits=[t_tr])
            bank_free(psA, 0, 512, t_ptk)
            for j in range(TOPK):
                col = tt * TOPK + j
                ej = topidx[:, col:col + 1]
                b.op("V", lambda v, ej=ej: v.tensor_scalar(
                    out=gcl[:, 0:E], in0=iota_e[:], scalar1=ej, scalar2=None,
                    op0=ALU.is_equal))
                b.op("V", lambda v: v.tensor_mul(gcl[:, 0:E], gcl[:, 0:E],
                                                 eL_sb[:]))
                b.op("V", lambda v: v.reduce_sum(sl_t, gcl[:, 0:E], axis=AX.X))
                b.op("V", lambda v, ej=ej: v.scalar_tensor_tensor(
                    out=g_t, in0=ej, scalar=float(CAP), in1=sl_t,
                    op0=ALU.mult, op1=ALU.add))
                b.op("V", lambda v: v.tensor_scalar_add(g_t, g_t, -1.0))
                b.op("V", lambda v: v.tensor_scalar(
                    out=ovf_t, in0=sl_t, scalar1=float(CAP + 1), scalar2=None,
                    op0=ALU.is_ge))
                b.op("V", lambda v: v.scalar_tensor_tensor(
                    out=g_t, in0=ovf_t, scalar=2.0e9, in1=g_t,
                    op0=ALU.mult, op1=ALU.add))
                t_idx_done = b.op("V", lambda v, col=col: v.tensor_copy(
                    idx_sb[:, col:col + 1], g_t))

        wv_view = lnscr2[:, D:D + 2 * E].bitcast(F32)
        ti_view = lnscr2[:, D + 2 * E:D + 2 * E + 2].bitcast(mybir.dt.int32)
        b.op("V", lambda v: v.memset(lnscr2[:, D + 2 * E + 2:RW], 0.0),
             waits=wmC)
        t_sca = None
        for tt in range(NQ):
            ttr = None
            for cc in range(NCH):
                ttr = b.op("T", lambda t_, cc=cc, tt=tt: t_.transpose(
                    psum_bf(psA, 0, 2048)[:, cc * 128:(cc + 1) * 128],
                    ln2_xT[:, cc * OWN + tt * 128: cc * OWN + (tt + 1) * 128],
                    ident_bf[:]),
                    waits=([t_idx_done, bank_tok.get((id(psA), 0)),
                            bank_tok.get((id(psA), 1))] if cc == 0 else []),
                    inc=(cc == NCH - 1))
            pv = psum_bf(psA, 0, 2048)[:].rearrange("p (c t) -> p c t", t=128)
            ln2v = lnscr2[:, 0:D].rearrange("p (c t) -> p c t", t=128)
            t_ev = b.op("V", lambda v, pv=pv, ln2v=ln2v: v.tensor_copy(
                ln2v, pv[:, 0:NCH, :]),
                waits=[ttr, t_sca])
            bank_free(psA, 0, 1024, t_ev)
            b.op("V", lambda v, tt=tt: v.tensor_copy(
                wv_view, w_sb[:, tt * E:(tt + 1) * E]))
            t_row = b.op("V", lambda v, tt=tt: v.tensor_copy(
                ti_view, tokid_t[:, tt:tt + 1]))
            for j in range(TOPK):
                col = tt * TOPK + j
                t_sca = b.dma("G", "sca", None, None) if False else None
                if b.active == "G":
                    for tk_ in [t_row, t_pfx]:
                        b.wait(("G"), tk_)
                b.dmacnt["sca"] += 16
                if b.active == "G":
                    b.nc.gpsimd.indirect_dma_start(
                        out=xarena[:],
                        out_offset=bass.IndirectOffsetOnAxis(
                            ap=idx_sb[:, col:col + 1], axis=0),
                        in_=lnscr2[:],
                        in_offset=None,
                        bounds_check=bc_x_reg,
                        oob_is_err=False,
                    ).then_inc(env["dsems"]["sca"], 16)
                t_sca = (env["dsems"]["sca"], b.dmacnt["sca"])
        t_scat_all = t_sca

        gu_read = {}
        wd_read = {}

        def issue_gu(k, first_waits=()):
            e_, i_ = divmod(k, NI)
            if e_ >= E:
                return None
            sl = k % 3
            wgp, wup = gusl[sl]
            src_g = wg_h[e_].rearrange("(c p) n -> p c n", p=128)[:, :, i_ * 128:(i_ + 1) * 128]
            src_u = wu_h[e_].rearrange("(c p) n -> p c n", p=128)[:, :, i_ * 128:(i_ + 1) * 128]
            war = list(first_waits) + [gu_read.get(k - 3)]
            half = max(1, NCH // 2)
            tk = None
            for piece in range(0, NCH, half):
                pe_ = min(piece + half, NCH)
                tk = b.dma("G", f"gu{sl}", wgp[:, piece:pe_, :],
                           src_g[:, piece:pe_, :], waits=war)
                war = []
            return tk

        def issue_wd(k, first_waits=()):
            e_, j_ = divmod(k, NDS)
            if e_ >= E:
                return None
            sl = k % 2
            src_d = wd_h[e_].rearrange("(c p) n -> p c n", p=128)[:, :, j_ * DS:(j_ + 1) * DS]
            return b.dma("G", f"wd{sl}", wdsl[sl][:], src_d,
                         waits=list(first_waits) + [wd_read.get(k - 2)])

        gu_tok = {}
        wd_tok = {}
        wu_dma = {}
        wu_cast = {}

        def issue_wu(k, first_waits=()):
            e_, i_ = divmod(k, NI)
            if e_ >= E:
                return
            half = NCH // 2
            src_u = wu_h[e_].rearrange("(c p) n -> p c n", p=128)[:, :, i_ * 128:(i_ + 1) * 128]
            wuts = []
            for hi in range(2):
                st = wustg[(k % 2) * 2 + hi]
                wuts.append(b.dma("S", f"wu{(k % 2) * 2 + hi}", st[:],
                                  src_u[:, hi * half:(hi + 1) * half, :],
                                  waits=list(first_waits)))
            wu_dma[k] = wuts
        for k in range(3):
            gu_tok[k] = issue_gu(k, first_waits=wmC + [t_scat_all])
        for k in range(2):
            issue_wu(k, first_waits=wmC + [t_scat_all])
        for k in range(2):
            wd_tok[k] = issue_wd(k, first_waits=wmC)

        xav2 = xarena.rearrange("(e t p) c -> e p t c", p=128, t=NCT)
        xtv = xtT[:].rearrange("p (c s) -> p c s", s=CAP)
        t_xt_done = {}
        t_gu_mm = {}
        t_y_read = {}
        ya_tok = {}
        t_wtib = {}
        ysc_n = [0]
        t_act_read = {}
        for e in range(E):
            t_xl = b.dma("Y", "xld", xt_sb[:], xav2[e],
                         waits=[t_scat_all, t_xt_done.get(e - 1)]
                         + (t_wtib.get(e - 1) or []))
            for tile in range(NCT):
                ttr = None
                for cc in range(NCH):
                    ttr = b.op("T", lambda t_, cc=cc, tile=tile: t_.transpose(
                        psum_bf(psA, 0, 2048)[:, cc * 128:(cc + 1) * 128],
                        xt_sb[:, tile, cc * 128:(cc + 1) * 128], ident_bf[:]),
                        waits=([t_xl, bank_tok.get((id(psA), 0)),
                                bank_tok.get((id(psA), 1))] if cc == 0 else []),
                        inc=(cc == NCH - 1))
                pv = psum_bf(psA, 0, 2048)[:].rearrange("p (c t) -> p c t",
                                                        t=128)
                tev = b.op("V", lambda v, pv=pv, tile=tile: v.tensor_copy(
                    xtv[:, :, tile * 128:(tile + 1) * 128], pv[:, 0:NCH, :]),
                    waits=[ttr, t_gu_mm.get(e - 1)])
                bank_free(psA, 0, 1024, tev)
            t_xt_done[e] = b.tok("T")
            for tile in range(NCT):
                b.op("V", lambda v, tile=tile: v.tensor_copy(
                    wbuf[:, tile:tile + 1],
                    xt_sb[:, tile, D + 2 * e:D + 2 * e + 2].bitcast(F32)),
                    waits=[t_xl] + (ya_tok.get(e - 1) or []))
                b.op("V", lambda v, tile=tile: v.tensor_copy(
                    tibuf[:, tile:tile + 1],
                    xt_sb[:, tile, D + 2 * E:D + 2 * E + 2].bitcast(
                        mybir.dt.int32)))
            t_wtib[e] = [b.tok("V")]
            t_xtT_ready = b.tok("V")
            for i in range(NI):
                k = e * NI + i
                sl = k % 3
                wgp, wup = gusl[sl]
                par = i % 2
                colg, colu = (0, 512) if par == 0 else (1024, 1536)
                tcast = None
                for hi in range(2):
                    st = wustg[(k % 2) * 2 + hi]
                    tcast = b.op("S", lambda s, st=st, hi=hi, wup=wup:
                                 s.activation(
                                     wup[:, hi * (NCH // 2):(hi + 1) * (NCH // 2), :],
                                     st[:], AF.Copy),
                                 waits=wu_dma.get(k, []) + [gu_read.get(k - 3)])
                wu_cast[k] = tcast
                issue_wu(k + 2)
                tg_ = mmgroup(psA, colg, CAP if CAP <= 512 else 512,
                              [wgp[:, cc, :] for cc in range(NCH)],
                              [xtv[:, cc, 0:CAP] for cc in range(NCH)],
                              waits=[gu_tok.get(k), t_xtT_ready])
                tu_ = mmgroup(psA, colu, CAP,
                              [wup[:, cc, :] for cc in range(NCH)],
                              [xtv[:, cc, 0:CAP] for cc in range(NCH)],
                              waits=[tcast])
                bcg = bgT_all[:, i * E + e:i * E + e + 1]
                bcu = buT_all[:, i * E + e:i * E + e + 1]
                t1 = b.op("V", lambda v, colg=colg, bcg=bcg: v.tensor_scalar(
                    out=gcl[:], in0=psA[:, colg:colg + CAP], scalar1=bcg,
                    scalar2=7.0, op0=ALU.add, op1=ALU.min), waits=[tg_])
                t2 = b.op("S", lambda s: s.activation(
                    sig[:], gcl[:], AF.Sigmoid, scale=1.702), waits=[t1])
                b.op("V", lambda v, colu=colu, bcu=bcu: v.tensor_scalar(
                    out=ucl[:], in0=psA[:, colu:colu + CAP], scalar1=bcu,
                    scalar2=7.0, op0=ALU.add, op1=ALU.min), waits=[tu_])
                t4_ = b.op("V", lambda v: v.tensor_scalar_max(
                    ucl[:], ucl[:], -7.0))
                t5 = b.op("V", lambda v: v.tensor_mul(gcl[:], gcl[:], sig[:]),
                          waits=[t2])
                b.op("V", lambda v, i=i: v.scalar_tensor_tensor(
                    out=actT[:, i * CAP:(i + 1) * CAP],
                    in0=ucl[:], scalar=1.0, in1=gcl[:],
                    op0=ALU.add, op1=ALU.mult),
                    waits=[t_act_read.get(e - 1)])
                bank_free(psA, colg, CAP, t5)
                bank_free(psA, colu, CAP, t4_)
                gu_read[k] = b.tok("T")
                gu_tok[k + 3] = issue_gu(k + 3)
            t_gu_mm[e] = b.tok("T")
            t_act_ready = b.tok("V")
            yr_toks = []
            t_ev_tile = [None] * NCT
            for j in range(NDS):
                k = e * NDS + j
                tk_d = wd_tok.get(k)
                sl = k % 2
                for tile in range(NCT):
                    pc = (tile % 4) * 512
                    tk = mmgroup(psB, pc, DS,
                                 [actT[:, i2 * CAP + tile * 128:
                                       i2 * CAP + (tile + 1) * 128]
                                  for i2 in range(NI)],
                                 [wdsl[sl][:, i2, :] for i2 in range(NI)],
                                 waits=[tk_d, t_act_ready])
                    yb = ysbt[tile]
                    wcol = wbuf[:, tile:tile + 1]
                    par = tile % 2
                    tev = b.op("V", lambda v, pc=pc, yb=yb, wcol=wcol, j=j:
                               v.tensor_scalar(
                                   out=yb[:, j * DS:(j + 1) * DS],
                                   in0=psB[:, pc:pc + DS],
                                   scalar1=wcol, scalar2=None, op0=ALU.mult),
                               waits=[tk]
                               + ([(env["dsems"][f"ya{par}"],
                                    b.dmacnt[f"ya{par}"])] if j == 0 else []))
                    bank_free(psB, pc, DS, tev)
                    yr_toks.append(tev)
                    t_ev_tile[tile] = tev
                wd_read[k] = b.tok("T")
                wd_tok[k + 2] = issue_wd(k + 2)
            for tile in range(NCT):
                ti_v = tibuf[:, tile:tile + 1]
                par = tile % 2
                if b.active == "G":
                    b.wait("G", t_ev_tile[tile])
                    b.wait("G", t_pfy)
                    b.wait("G", (env["dsems"]["ya0"], b.dmacnt["ya0"]))
                    b.wait("G", (env["dsems"]["ya1"], b.dmacnt["ya1"]))
                b.dmacnt[f"ya{par}"] += 16
                if b.active == "G":
                    b.nc.gpsimd.indirect_dma_start(
                        out=yarena,
                        out_offset=bass.IndirectOffsetOnAxis(
                            ap=ti_v, axis=0),
                        in_=ysbt[tile][:],
                        in_offset=None,
                        bounds_check=bc_y_reg,
                        oob_is_err=False,
                        compute_op=ALU.add,
                    ).then_inc(env["dsems"][f"ya{par}"], 16)
            t_act_read[e] = b.tok("T")
            t_y_read[e] = yr_toks[-2:]
            ya_tok[e] = [(env["dsems"]["ya0"], b.dmacnt["ya0"]),
                         (env["dsems"]["ya1"], b.dmacnt["ya1"])]
        t_ya_all = [(env["dsems"]["ya0"], b.dmacnt["ya0"]),
                    (env["dsems"]["ya1"], b.dmacnt["ya1"])]

        t_cv = None
        for tt in range(NQ):
            t_yl = b.dma("Y", "yrd", ysbt[0][:], yav[tt],
                         waits=t_ya_all + [t_cv])
            t_cv = b.op("V", lambda v: v.tensor_copy(tmp_f[:], ysbt[0][:]),
                        waits=[t_yl])
            b.op("V", lambda v, tt=tt: v.tensor_add(
                resid[:, tt * D:(tt + 1) * D], resid[:, tt * D:(tt + 1) * D],
                tmp_f[:]))

        wmE = b.wm()
        st_tok = None
        for q in range(NQ):
            st_tok = b.dma("Y", "st", y_h[q * 128:(q + 1) * 128, :],
                           resid[:, q * D:(q + 1) * D], waits=wmE)
        if b.active == "Y":
            b.nc.sync.wait_ge(env["dsems"]["st"], st_tok[1])

    with es_glob:
        with nc.Block() as blk:
            @blk.tensor
            def _(t):
                walk(EB(nc, "T", env))

            @blk.vector
            def _(v):
                walk(EB(nc, "V", env))

            @blk.scalar
            def _(s):
                walk(EB(nc, "S", env))

            @blk.gpsimd
            def _(g):
                walk(EB(nc, "G", env))

            @blk.sync
            def _(y):
                walk(EB(nc, "Y", env))
    return nc


def kernel(L0_attn_x, L0_attn_Wq, L0_attn_Wk, L0_attn_Wv, L0_attn_Wo,
           L0_attn_norm_scale, L0_attn_lam, L0_ffn_gate_w, L0_ffn_gate_b,
           L0_ffn_w_gate, L0_ffn_b_gate, L0_ffn_w_up, L0_ffn_b_up,
           L0_ffn_w_down, L0_ffn_b_down):
    x = np.ascontiguousarray(np.asarray(L0_attn_x, dtype=np.float32))
    B, S, Dm = x.shape
    nc = build_program(CFG_FULL)
    f32c = lambda a: np.ascontiguousarray(np.asarray(a, dtype=np.float32))
    shared = {
        "Wq": f32c(L0_attn_Wq), "Wk": f32c(L0_attn_Wk), "Wv": f32c(L0_attn_Wv),
        "Wo": f32c(L0_attn_Wo), "norm_scale": f32c(L0_attn_norm_scale),
        "lam": f32c(L0_attn_lam), "gate_w": f32c(L0_ffn_gate_w),
        "gate_b": f32c(L0_ffn_gate_b), "w_gate": f32c(L0_ffn_w_gate),
        "b_gate": f32c(L0_ffn_b_gate), "w_up": f32c(L0_ffn_w_up),
        "b_up": f32c(L0_ffn_b_up), "w_down": f32c(L0_ffn_w_down),
        "b_down": f32c(L0_ffn_b_down),
    }
    in_maps = []
    half = S // 2
    for core in range(8):
        bidx, h = core // 2, core % 2
        m = dict(shared)
        m["x_own"] = np.ascontiguousarray(x[bidx, h * half:(h + 1) * half])
        m["x_part"] = np.ascontiguousarray(x[bidx, (1 - h) * half:(2 - h) * half])
        in_maps.append(m)
    res = run_bass_kernel_spmd(nc, in_maps, core_ids=list(range(8)))
    y = np.empty((B, S, Dm), dtype=np.float32)
    for core in range(8):
        bidx, h = core // 2, core % 2
        y[bidx, h * half:(h + 1) * half] = res.results[core]["y_own"]
    return y


# version 20
# speedup vs baseline: 1.1047x; 1.1047x over previous
import contextlib
import numpy as np
import ml_dtypes

import concourse.bass as bass
import concourse.mybir as mybir
from concourse.bass_utils import run_bass_kernel_spmd

F32 = mybir.dt.float32
BF16 = mybir.dt.bfloat16
AF = mybir.ActivationFunctionType
ALU = mybir.AluOpType
AX = mybir.AxisListType

SBUF_BASE = 16512

CFG_FULL = dict(d_model=2048, own=1024, part=1024, heads=16, experts=32,
                topk=4, dff=1024)

DETECT_RACES = True


def derive(cfg):
    c = dict(cfg)
    c["NCH"] = cfg["d_model"] // 128
    c["NQ"] = cfg["own"] // 128
    c["NT"] = (cfg["own"] + cfg["part"]) // 128
    c["T"] = cfg["own"] + cfg["part"]
    c["NI"] = cfg["dff"] // 128
    c["ST"] = min(512, cfg["own"])
    c["NST"] = cfg["own"] // c["ST"]
    c["DS"] = min(512, cfg["d_model"])
    c["NDS"] = cfg["d_model"] // c["DS"]
    c["QS"] = 1.0 / float(np.sqrt(np.float32(128)))
    mean_cnt = cfg["own"] * cfg["topk"] // cfg["experts"]
    c["CAP"] = min(384, max(128, ((3 * mean_cnt + 127) // 128) * 128))
    c["NCT"] = c["CAP"] // 128
    c["RW"] = ((cfg["d_model"] + 2 * cfg["experts"] + 2 + 15) // 16) * 16
    return c


class EB:

    ENGS = ("T", "V", "S", "G", "Y")

    def __init__(self, nc, active, env):
        self.nc = nc
        self.active = active
        self.env = env
        self.cnt = {e: 0 for e in self.ENGS}
        self.dmacnt = dict.fromkeys(env["dsems"], 0)
        self.waited = {}

    def _eng(self, e):
        nc = self.nc
        return {"T": nc.tensor, "V": nc.vector, "S": nc.scalar,
                "G": nc.gpsimd, "Y": nc.sync}[e]

    def wait(self, e, tok):
        if tok is None or self.active != e:
            return
        sem, val = tok
        if val <= 0:
            return
        key = id(sem)
        if self.waited.get(key, -1) < val:
            self._eng(e).wait_ge(sem, val)
            self.waited[key] = val

    def op(self, e, fn, waits=(), inc=True):
        if self.active == e:
            for tk in waits:
                self.wait(e, tk)
            if e in ("V", "S", "G") and self.cnt[e] > 0:
                self.wait(e, (self.env["sems"][e], self.cnt[e]))
            ins = fn(self._eng(e))
            if inc:
                ins.then_inc(self.env["sems"][e], 1)
        if inc:
            self.cnt[e] += 1
            return (self.env["sems"][e], self.cnt[e])
        return None

    def tok(self, e):
        return (self.env["sems"][e], self.cnt[e])

    def dma(self, q, sem_name, out, in_, waits=()):
        sem = self.env["dsems"][sem_name]
        self.dmacnt[sem_name] += 16
        if self.active == q:
            for tk in waits:
                self.wait(q, tk)
            self._eng(q).dma_start(out=out, in_=in_).then_inc(sem, 16)
        return (sem, self.dmacnt[sem_name])

    def wm(self):
        return [(self.env["sems"][e], self.cnt[e]) for e in ("T", "V", "S")]


def build_program(cfg):
    c = derive(cfg)
    D, H, E, I = cfg["d_model"], cfg["heads"], cfg["experts"], cfg["dff"]
    NCH, NQ, NT, NI = c["NCH"], c["NQ"], c["NT"], c["NI"]
    ST, NST, DS, NDS = c["ST"], c["NST"], c["DS"], c["NDS"]
    T = c["T"]
    OWN = cfg["own"]
    TOPK = cfg["topk"]
    CAP, NCT, RW = c["CAP"], c["NCT"], c["RW"]
    CAPTOT = E * CAP

    nc = bass.Bass("TRN2", target_bir_lowering=False,
                   detect_race_conditions=DETECT_RACES)

    xo = nc.dram_tensor("x_own", [OWN, D], F32, kind="ExternalInput").ap()
    xp = nc.dram_tensor("x_part", [cfg["part"], D], F32, kind="ExternalInput").ap()
    Wq = nc.dram_tensor("Wq", [D, H * 256], F32, kind="ExternalInput").ap()
    Wk = nc.dram_tensor("Wk", [D, H * 256], F32, kind="ExternalInput").ap()
    Wv = nc.dram_tensor("Wv", [D, H * 128], F32, kind="ExternalInput").ap()
    Wo = nc.dram_tensor("Wo", [H * 128, D], F32, kind="ExternalInput").ap()
    ns_h = nc.dram_tensor("norm_scale", [D], F32, kind="ExternalInput").ap()
    lam_h = nc.dram_tensor("lam", [H], F32, kind="ExternalInput").ap()
    gw_h = nc.dram_tensor("gate_w", [E, D], F32, kind="ExternalInput").ap()
    gb_h = nc.dram_tensor("gate_b", [E], F32, kind="ExternalInput").ap()
    wg_h = nc.dram_tensor("w_gate", [E, D, I], F32, kind="ExternalInput").ap()
    bg_h = nc.dram_tensor("b_gate", [E, I], F32, kind="ExternalInput").ap()
    wu_h = nc.dram_tensor("w_up", [E, D, I], F32, kind="ExternalInput").ap()
    bu_h = nc.dram_tensor("b_up", [E, I], F32, kind="ExternalInput").ap()
    wd_h = nc.dram_tensor("w_down", [E, I, D], F32, kind="ExternalInput").ap()
    bd_h = nc.dram_tensor("b_down", [E, D], F32, kind="ExternalInput").ap()
    y_h = nc.dram_tensor("y_own", [OWN, D], F32, kind="ExternalOutput").ap()

    id_bf_h = nc.inline_tensor(
        np.eye(128, dtype=np.float32).astype(ml_dtypes.bfloat16), name="idbf").ap()
    id_f_h = nc.inline_tensor(np.eye(128, dtype=np.float32), name="idf").ap()
    iota_e_h = nc.inline_tensor(
        np.tile(np.arange(E, dtype=np.float32), (128, 1)), name="iotae").ap()
    tokid_h = nc.inline_tensor(
        (np.arange(NQ, dtype=np.int32)[None, :] * 128
         + np.arange(128, dtype=np.int32)[:, None]).copy(), name="tokid").ap()
    xarena = nc.dram_tensor("xarena", [CAPTOT, RW], BF16, kind="Internal").ap()
    yarena = nc.dram_tensor("yarena", [OWN, D], BF16, kind="Internal").ap()

    def at(name, shape, dt, off):
        return nc.alloc_sbuf_tensor_at(
            name, shape, dt, offset=SBUF_BASE + off).ap()

    KB = 1024
    A32 = lambda x: (x + 31) & ~31
    cur = [0]

    def nxt(nbytes):
        off = A32(cur[0])
        cur[0] = off + nbytes
        return off

    ident_bf = at("ident_bf", [128, 128], BF16, nxt(256))
    ident_f = at("ident_f", [128, 128], F32, nxt(512))
    ones_col = at("ones_col", [1, 128], F32, nxt(512))
    ones_row = at("ones_row", [1, 512], F32, nxt(2048))
    lam_b = at("lam_b", [128, H], F32, nxt(H * 4))
    stats = at("stats", [128, 64], F32, nxt(256))
    gwT = at("gwT", [128, NCH * 32], BF16, nxt(NCH * 64))
    gb_row = at("gb_row", [1, E], F32, nxt(E * 4))
    bgT_all = at("bgT_all", [128, NI * E], F32, nxt(NI * E * 4))
    buT_all = at("buT_all", [128, NI * E], F32, nxt(NI * E * 4))
    bd_all = at("bd_all", [E, D], F32, nxt(D * 4))
    ns_row = at("ns_row", [1, D], F32, nxt(D * 4))
    iota_e = at("iota_e", [128, E], F32, nxt(E * 4))
    tokid_t = at("tokid_t", [128, NQ], mybir.dt.int32, nxt(NQ * 4))
    assert cur[0] <= 24 * KB, cur[0]

    R1, R2, R3, R4, R5 = 24 * KB, 88 * KB, 120 * KB, 160 * KB, 200 * KB
    ln_xT = at("ln_xT", [128, NCH * T], BF16, R1)
    resid = at("resid", [128, NQ * D], F32, R1)
    assert max(NCH * T * 2, NQ * D * 4) <= R2 - R1
    stage_f = at("stage_f", [E, max(I, D)], F32, R2)
    gw_sb = at("gw_sb", [E, D], BF16, R2 + max(I, D) * 4)
    out_sb = at("out_sb", [128, NQ * D], BF16, R2)
    ln2_xT = at("ln2_xT", [128, NCH * OWN], BF16, R2)
    assert NQ * D * 2 <= R3 - R2 and NCH * OWN * 2 <= R3 - R2
    e1 = at("e1", [128, T], F32, R3)
    e2 = at("e2", [128, T], F32, R3 + T * 4)
    o = R3 + 2 * T * 4
    k1T = at("k1T", [128, T], BF16, o); o += T * 2
    k2T = at("k2T", [128, T], BF16, o); o += T * 2
    v_sb = at("v_sb", [128, T], BF16, o); o += T * 2
    q1T = at("q1T", [128, OWN], BF16, o); o += OWN * 2
    q2T = at("q2T", [128, OWN], BF16, o); o += OWN * 2
    p_sb = at("p_sb", [128, T], BF16, o); o += T * 2
    pt_sb = at("pt_sb", [128, T], BF16, o); o += T * 2
    assert o <= R4, o
    out_nT = at("out_nT", [128, NCH * OWN], BF16, R3)
    ns_b = at("ns_b", [128, D], F32, R3 + NCH * OWN * 2)
    assert R3 + NCH * OWN * 2 + D * 4 <= R4
    actT = at("actT", [128, NI * CAP], BF16, R3)
    dof = R3 + NI * CAP * 2
    def nx2(n):
        nonlocal_holder[0] = (nonlocal_holder[0] + 31) & ~31
        off = nonlocal_holder[0]
        nonlocal_holder[0] = off + n
        return off
    nonlocal_holder = [dof]
    xtT = at("xtT", [128, NCH * CAP], BF16, nx2(NCH * CAP * 2))
    w_sb = at("w_sb", [128, NQ * E], F32, nx2(NQ * E * 4))
    lnscr = at("lnscr", [128, D], BF16, nx2(D * 2))
    wT_sb = at("wT_sb", [E, OWN], F32, nx2(OWN * 4))
    eL_sb = at("eL_sb", [128, E], F32, nx2(E * 4))
    gcl = at("gcl", [128, CAP], F32, nx2(CAP * 4))
    ucl = at("ucl", [128, CAP], F32, nx2(CAP * 4))
    sig = at("sig", [128, CAP], F32, nx2(CAP * 4))
    idx_sb = at("idx_sb", [128, NQ * TOPK], mybir.dt.int32, nx2(NQ * TOPK * 4))
    topidx = at("topidx", [128, NQ * TOPK], F32, nx2(NQ * TOPK * 4))
    idx8 = at("idx8", [128, 8], mybir.dt.uint32, nx2(32))
    wbuf = at("wbuf", [128, 4], F32, nx2(16))
    tibuf = at("tibuf", [128, 4], mybir.dt.int32, nx2(16))
    ysb_r3 = [at(f"ysbr{i}", [128, D], BF16, nx2(D * 2))
              for i in range(max(0, NCT - 2))]
    dof = nonlocal_holder[0]
    assert dof <= R4, dof
    xt_sb = at("xt_sb", [128, NCT, RW], BF16, R2)
    junk_d = at("junk_d", [128, D], BF16, R3)
    ysbt = []
    for i in range(min(2, NCT)):
        ysbt.append(at(f"ysb{i}", [128, D], BF16,
                       R2 + NCT * RW * 2 + i * D * 2))
    assert NCT * RW * 2 + min(2, NCT) * D * 2 <= R3 - R2
    ysbt.extend(ysb_r3)
    posT = at("posT", [E, OWN], F32, R5)
    maskT = at("maskT", [E, OWN], F32, R4 + 36 * KB)
    assert R5 + OWN * 4 <= 212832
    zt = at("zt", [128, RW], BF16, R3)
    lnscr2 = at("lnscr2", [128, RW], BF16, R4)
    xsl = [at(f"xsl{i}", [128, D], F32, R4 + i * D * 4) for i in range(2)]
    WQKB = NCH * 1024 + NCH * 256
    wqkv = []
    for sl in range(2):
        base = R4 + sl * WQKB
        wqkv.append((at(f"wq{sl}", [128, NCH, 256], BF16, base),
                     at(f"wk{sl}", [128, NCH, 256], BF16, base + NCH * 512),
                     at(f"wv{sl}", [128, NCH, 128], BF16, base + NCH * 1024)))
    assert 2 * WQKB <= 40 * KB
    wosl = [at(f"wo{sl}", [128, NCH, DS], BF16, R4 + sl * NCH * DS * 2)
            for sl in range(2)]
    assert 2 * NCH * DS * 2 <= 40 * KB
    gusl = [(at(f"wgp{sl}", [128, NCH, 128], BF16, R4 + sl * NCH * 512),
             at(f"wup{sl}", [128, NCH, 128], BF16,
                R4 + sl * NCH * 512 + NCH * 256)) for sl in range(3)]
    wdsl = [at(f"wdp{sl}", [128, NI, DS], BF16,
               R4 + 3 * NCH * 512 + sl * NI * DS * 2) for sl in range(2)]
    assert 3 * NCH * 512 + 2 * NI * DS * 2 <= 40 * KB
    junk_bf = at("junk_bf", [128, D], BF16, R4 + 32 * KB)
    tmp_f = at("tmp_f", [128, D], F32, R4 + 36 * KB)
    xsl2 = [at(f"xc{i}", [128, DS], F32, R5 + 2 * KB + i * DS * 4)
            for i in range(2)]
    assert R5 + 2 * KB + 2 * DS * 4 <= 212832

    es_glob = contextlib.ExitStack()
    psA = es_glob.enter_context(nc.psum_tensor("psA", [128, 2048], F32))
    psB = es_glob.enter_context(nc.psum_tensor("psB", [128, 2048], F32))

    sems = {e: es_glob.enter_context(nc.semaphore(f"sem{e}")) for e in EB.ENGS}
    dnames = (["cstY", "cstG", "st", "pfx", "pfy", "sca", "xld", "ya0", "ya1", "yrd"]
              + [f"xsl{i}" for i in range(2)]
              + [f"wqk{i}" for i in range(2)]
              + [f"wo{i}" for i in range(2)]
              + [f"gu{i}" for i in range(3)]
              + [f"wd{i}" for i in range(2)]
              + [f"xc{i}" for i in range(2)])
    dsems = {n: es_glob.enter_context(nc.semaphore(f"d_{n}")) for n in dnames}
    env = {"sems": sems, "dsems": dsems}

    EPS_LN = 1e-5
    EPS_RMS = 1e-6

    def psum_bf(ps, col0_f32, ncols_bf):
        return ps[:, col0_f32:col0_f32 + (ncols_bf + 1) // 2].bitcast(BF16)

    def chunks(apT, n_per, width):
        return apT[:].rearrange("p (c t) -> p c t", t=n_per)

    def walk(b: EB):
        bank_tok = {}

        def mmgroup(ps, col0, n, lhsTs, rhss, waits=()):
            tk = None
            nmm = len(lhsTs)
            for i, (lh, rh) in enumerate(zip(lhsTs, rhss)):
                w = []
                if i == 0:
                    w = list(waits)
                    for cb in range(col0 // 512, (col0 + n + 511) // 512):
                        w.append(bank_tok.get((id(ps), cb)))
                tk = b.op("T", lambda t_, lh=lh, rh=rh, i=i: t_.matmul(
                    ps[:, col0:col0 + n], lh, rh,
                    start=(i == 0), stop=(i == nmm - 1)),
                    waits=w, inc=(i == nmm - 1))
            return tk

        def bank_free(ps, col0, n, tok):
            for cb in range(col0 // 512, (col0 + n + 511) // 512):
                bank_tok[(id(ps), cb)] = tok

        b.dma("Y", "cstY", ident_bf[:], id_bf_h[:])
        b.dma("Y", "cstY", ident_f[:], id_f_h[:])
        b.dma("Y", "cstY", lam_b[0:1, :], lam_h[None, :])
        b.dma("Y", "cstY", ns_row[:], ns_h[None, :])
        b.dma("Y", "cstY", gb_row[:], gb_h[None, :])
        b.dma("Y", "cstY", bd_all[:], bd_h[:])
        t_gw = b.dma("G", "cstG", gw_sb[:], gw_h[:])
        b.dma("Y", "cstY", iota_e[:], iota_e_h[:])
        b.dma("Y", "cstY", tokid_t[:], tokid_h[:])
        t_bg = b.dma("Y", "cstY", stage_f[0:E, 0:I], bg_h[:])
        tok_cy = t_bg
        t_id1 = t_id2 = t_lam = tok_cy
        t_m1 = b.op("V", lambda v: v.memset(ones_col[:], 1.0))
        b.op("V", lambda v: v.memset(ones_row[:], 1.0))
        eps_ln = stats[:, 21:22]
        eps_rms = stats[:, 22:23]
        b.op("V", lambda v: v.memset(eps_ln, EPS_LN))
        b.op("V", lambda v: v.memset(eps_rms, EPS_RMS))
        b.op("V", lambda v: v.memset(zt[:], 0.0))
        t_zt = b.op("V", lambda v: v.memset(
            zt[:, D + 2 * E:D + 2 * E + 2].bitcast(mybir.dt.int32), OWN))
        tk = b.op("T", lambda t_: t_.matmul(psA[:, 0:H], ones_col[:],
                                            lam_b[0:1, :], start=True, stop=True),
                  waits=[t_lam, t_m1])
        t_lb = b.op("V", lambda v: v.tensor_copy(lam_b[:], psA[:, 0:H]),
                    waits=[tk])
        one_minus = stats[:, 20:21]
        b.op("V", lambda v: v.reduce_sum(one_minus, lam_b[:], axis=AX.X),
             waits=[t_lb])
        b.op("V", lambda v: v.tensor_scalar(
            out=one_minus, in0=one_minus, scalar1=-1.0 / H, scalar2=1.0,
            op0=ALU.mult, op1=ALU.add))
        gw_ev = []
        for cc in range(NCH):
            tt = b.op("T", lambda t_, cc=cc: t_.transpose(
                psum_bf(psB, (cc % 4) * 512, 2 * E)[:, 0:E],
                gw_sb[0:E, cc * 128:(cc + 1) * 128], ident_bf[0:E, 0:E]),
                waits=[t_gw, t_id1] + ([gw_ev[cc - 4]] if cc >= 4 else []))
            gw_ev.append(b.op("V", lambda v, cc=cc, tt=tt: v.tensor_copy(
                gwT[:, cc * 32:cc * 32 + E],
                psum_bf(psB, (cc % 4) * 512, 2 * E)[:, 0:E]), waits=[tt]))

        def bias_table(dst, t_loaded):
            toks = []
            for i in range(NI):
                col = 512 + (i % 2) * 512
                tt = b.op("T", lambda t_, i=i, col=col: t_.transpose(
                    psA[0:128, col:col + E],
                    stage_f[0:E, i * 128:(i + 1) * 128], ident_f[0:E, 0:E]),
                    waits=[t_loaded, t_id2] + ([toks[i - 2]] if i >= 2 else []))
                toks.append(b.op("V", lambda v, i=i, col=col, tt=tt: v.tensor_copy(
                    dst[:, i * E:(i + 1) * E], psA[:, col:col + E]), waits=[tt]))
            return toks
        tg_tab = bias_table(bgT_all, t_bg)
        t_bu = b.dma("Y", "cstY", stage_f[0:E, 0:I], bu_h[:],
                     waits=[tg_tab[-1]])
        tu_tab = bias_table(buT_all, t_bu)
        init_v = b.tok("V")
        bank_free(psA, 0, 2048, init_v)
        bank_free(psB, 0, 2048, init_v)

        negm = stats[:, 0:1]
        sq = stats[:, 1:2]
        var = stats[:, 2:3]
        m2c = stats[:, 3:4]
        rstd = stats[:, 4:5]
        t_xread = {}
        a_ev = init_v
        for tt in range(NT):
            src = xo if tt < NQ else xp
            row0 = (tt if tt < NQ else tt - NQ) * 128
            sl = tt % 2
            t_x = b.dma("Y", f"xsl{sl}", xsl[sl][:], src[row0:row0 + 128, :],
                        waits=[t_xread.get(tt - 2)])
            b.op("V", lambda v, sl=sl: v.reduce_sum(
                negm, xsl[sl][:], axis=AX.X, negate=True), waits=[t_x])
            b.op("V", lambda v: v.tensor_scalar_mul(negm, negm, 1.0 / D))
            t_sq = b.op("S", lambda s, sl=sl: s.activation(
                junk_bf[:], xsl[sl][:], AF.Square, accum_out=sq), waits=[t_x])
            b.op("V", lambda v: v.tensor_mul(m2c, negm, negm))
            b.op("V", lambda v: v.tensor_scalar(
                out=var, in0=sq, scalar1=1.0 / D, scalar2=None, op0=ALU.mult),
                waits=[t_sq])
            t_v3 = b.op("V", lambda v: v.tensor_sub(var, var, m2c))
            b.op("S", lambda s: s.activation(rstd, var, AF.Ln, bias=eps_ln),
                 waits=[t_v3])
            t_r2 = b.op("S", lambda s: s.activation(rstd, rstd, AF.Exp,
                                                    scale=-0.5))
            t_ln = b.op("V", lambda v, sl=sl: v.tensor_scalar(
                out=lnscr[:], in0=xsl[sl][:], scalar1=negm, scalar2=rstd,
                op0=ALU.add, op1=ALU.mult), waits=[t_r2])
            t_xread[tt] = t_ln
            ttr = None
            for cc in range(NCH):
                ttr = b.op("T", lambda t_, cc=cc: t_.transpose(
                    psum_bf(psA, 0, 2048)[:, cc * 128:(cc + 1) * 128],
                    lnscr[:, cc * 128:(cc + 1) * 128], ident_bf[:]),
                    waits=([t_ln, a_ev] if cc == 0 else []),
                    inc=(cc == NCH - 1))
            lnv = ln_xT[:].rearrange("p (c t) -> p c t", t=T)
            pv = psum_bf(psA, 0, 2048)[:].rearrange("p (c t) -> p c t", t=128)
            a_ev = b.op("V", lambda v, tt=tt, lnv=lnv, pv=pv: v.tensor_copy(
                lnv[:, :, tt * 128:(tt + 1) * 128], pv[:, 0:NCH, :]),
                waits=[ttr])
        bank_free(psA, 0, 1024, a_ev)
        wmA = b.wm()
        xav = xarena.rearrange("(n p) c -> n p c", p=128)
        t_pfx = None
        for blk in range(CAPTOT // 128):
            t_pfx = b.dma("Y", "pfx", xav[blk], zt[:], waits=[t_zt])
        yav = yarena.rearrange("(n p) c -> n p c", p=128)
        t_pfy = None
        for blk in range(OWN // 128):
            t_pfy = b.dma("Y", "pfy", yav[blk], zt[:, 0:D], waits=[t_zt])

        nm1 = stats[:, 5:6]
        nm2 = stats[:, 6:7]
        z1 = stats[:, 7:8]
        z2 = stats[:, 8:9]
        zd = stats[:, 9:10]
        r1s = stats[:, 10:11]
        b2s = stats[:, 11:12]
        rzd = stats[:, 12:13]
        lnv = ln_xT[:].rearrange("p (c t) -> p c t", t=T)
        t_wread = {}
        head_end = None
        for h in range(H):
            sl = h % 2
            wqs, wks, wvs = wqkv[sl]
            wq_src = Wq.rearrange("(c p) n -> p c n", p=128)[:, :, h * 256:(h + 1) * 256]
            wk_src = Wk.rearrange("(c p) n -> p c n", p=128)[:, :, h * 256:(h + 1) * 256]
            wv_src = Wv.rearrange("(c p) n -> p c n", p=128)[:, :, h * 128:(h + 1) * 128]
            war = [t_wread.get(h - 2)] + (wmA if h < 2 else [])
            half = max(1, NCH // 2)
            tw = None
            for piece in range(0, NCH, half):
                pe = min(piece + half, NCH)
                b.dma("G", f"wqk{sl}", wqs[:, piece:pe, :],
                      wq_src[:, piece:pe, :], waits=war)
                war = []
                b.dma("G", f"wqk{sl}", wks[:, piece:pe, :],
                      wk_src[:, piece:pe, :])
                tw = b.dma("G", f"wqk{sl}", wvs[:, piece:pe, :],
                           wv_src[:, piece:pe, :])
            nkb = max(1, T // 512)
            kb = T // nkb
            for comp, kT in enumerate((k1T, k2T)):
                wofs = comp * 128
                ps = psA if comp == 0 else psB
                for blk in range(nkb):
                    col = blk * kb
                    pc = (blk % 4) * 512
                    tk = mmgroup(ps, pc, kb,
                                 [wks[:, cc, wofs:wofs + 128] for cc in range(NCH)],
                                 [lnv[:, cc, col:col + kb] for cc in range(NCH)],
                                 waits=[tw, head_end] + wmA)
                    te = b.op("S", lambda s, kT=kT, col=col, ps=ps, pc=pc, kb=kb:
                              s.activation(kT[:, col:col + kb],
                                           ps[:, pc:pc + kb], AF.Copy),
                              waits=[tk])
                    bank_free(ps, pc, kb, te)
            nqb = max(1, OWN // 512)
            qb = OWN // nqb
            for comp, qT in enumerate((q1T, q2T)):
                wofs = comp * 128
                for blk in range(nqb):
                    col = blk * qb
                    pc = ((comp * nqb + blk) % 4) * 512
                    tk = mmgroup(psA, pc, qb,
                                 [wqs[:, cc, wofs:wofs + 128] for cc in range(NCH)],
                                 [lnv[:, cc, col:col + qb] for cc in range(NCH)],
                                 waits=[tw])
                    te = b.op("S", lambda s, qT=qT, col=col, pc=pc, qb=qb:
                              s.activation(qT[:, col:col + qb],
                                           psA[:, pc:pc + qb], AF.Copy,
                                           scale=c["QS"]), waits=[tk])
                    bank_free(psA, pc, qb, te)
            for ttile in range(NT):
                pc = (ttile % 4) * 512
                tk = mmgroup(psB, pc, 128,
                             [lnv[:, cc, ttile * 128:(ttile + 1) * 128]
                              for cc in range(NCH)],
                             [wvs[:, cc, :] for cc in range(NCH)],
                             waits=[tw])
                te = b.op("S", lambda s, ttile=ttile, pc=pc:
                          s.activation(v_sb[:, ttile * 128:(ttile + 1) * 128],
                                       psB[:, pc:pc + 128], AF.Copy),
                          waits=[tk])
                bank_free(psB, pc, 128, te)
            t_wread[h] = b.tok("T")
            nsb = max(1, T // 512)
            sbw = T // nsb
            pend = None
            last_diff = None
            last_pv = None
            last_ptr = None
            for q in range(NQ + 1):
                if q < NQ:
                    t_s1 = None
                    for blk in range(nsb):
                        w0 = ([bank_tok.get((id(psA), cb)) for cb in range(4)]
                              if blk == 0 else [])
                        t_s1 = b.op("T", lambda t_, blk=blk, q=q:
                                    t_.matmul(psA[:, blk * sbw:(blk + 1) * sbw],
                                              q1T[:, q * 128:(q + 1) * 128],
                                              k1T[:, blk * sbw:(blk + 1) * sbw],
                                              start=True, stop=True),
                                    waits=w0, inc=(blk == nsb - 1))
                    t_e1 = b.op("S", lambda s: s.activation(
                        e1[:], psA[:, 0:T], AF.Exp, accum_out=z1),
                        waits=[t_s1, last_diff]
                        + ([t_pfx, t_pfy] if h == 0 and q == 0 else []))
                    bank_free(psA, 0, 2048, t_e1)
                if pend is not None:
                    qp, dd = pend
                    ptv = psum_bf(psB, 0, 2048)
                    ttr = None
                    for tau in range(NT):
                        ttr = b.op("T", lambda t_, tau=tau, ptv=ptv: t_.transpose(
                            ptv[:, tau * 128:(tau + 1) * 128],
                            p_sb[:, tau * 128:(tau + 1) * 128], ident_bf[:]),
                            waits=([dd["t_pd"], bank_tok.get((id(psB), 0)),
                                    bank_tok.get((id(psB), 1))]
                                   if tau == 0 else []),
                            inc=(tau == NT - 1))
                    last_ptr = ttr
                    t_ptev = b.op("V", lambda v, ptv=ptv: v.tensor_copy(
                        pt_sb[:, 0:T], ptv[:, 0:T]), waits=[ttr, last_pv])
                    bank_free(psB, 0, 1024, t_ptev)
                if q < NQ:
                    t_s2 = None
                    for blk in range(nsb):
                        w0 = ([bank_tok.get((id(psA), cb)) for cb in range(4)]
                              if blk == 0 else [])
                        t_s2 = b.op("T", lambda t_, blk=blk, q=q:
                                    t_.matmul(psA[:, blk * sbw:(blk + 1) * sbw],
                                              q2T[:, q * 128:(q + 1) * 128],
                                              k2T[:, blk * sbw:(blk + 1) * sbw],
                                              start=True, stop=True),
                                    waits=w0, inc=(blk == nsb - 1))
                if pend is not None:
                    pvslot = 1024 + (qp % 2) * 512
                    t_pv = None
                    for tau in range(NT):
                        t_pv = b.op("T", lambda t_, tau=tau, pvslot=pvslot:
                                    t_.matmul(
                                        psB[:, pvslot:pvslot + 128],
                                        pt_sb[:, tau * 128:(tau + 1) * 128],
                                        v_sb[:, tau * 128:(tau + 1) * 128],
                                        start=(tau == 0), stop=(tau == NT - 1)),
                                    waits=([t_ptev,
                                            bank_tok.get((id(psB),
                                                          pvslot // 512))]
                                           if tau == 0 else []),
                                    inc=(tau == NT - 1))
                    last_pv = t_pv
                    b.op("V", lambda v: v.reciprocal(rzd, zd),
                         waits=[dd["t_pd"]])
                    t_pvev = b.op("V", lambda v, qp=qp, pvslot=pvslot, h=h:
                                  v.tensor_scalar(
                                      out=out_sb[:, qp * D + h * 128:
                                                 qp * D + (h + 1) * 128],
                                      in0=psB[:, pvslot:pvslot + 128],
                                      scalar1=rzd, scalar2=None,
                                      op0=ALU.mult), waits=[t_pv])
                    bank_free(psB, pvslot, 512, t_pvev)
                    head_end = t_pvev
                    pend = None
                if q == NQ:
                    break
                t_e2 = b.op("S", lambda s: s.activation(
                    e2[:], psA[:, 0:T], AF.Exp, accum_out=z2),
                    waits=[t_s2])
                bank_free(psA, 0, 2048, t_e2)
                b.op("V", lambda v: v.reciprocal(r1s, z1), waits=[t_e1])
                b.op("V", lambda v: v.reciprocal(b2s, z2), waits=[t_e2])
                b.op("V", lambda v, h=h: v.tensor_mul(b2s, b2s,
                                                      lam_b[:, h:h + 1]))
                b.op("V", lambda v: v.tensor_scalar_mul(e2[:], e2[:], b2s))
                t_diff = b.op("V", lambda v: v.scalar_tensor_tensor(
                    out=e2[:], in0=e1[:], scalar=r1s, in1=e2[:],
                    op0=ALU.mult, op1=ALU.subtract))
                last_diff = t_diff
                t_pd = b.op("S", lambda s: s.activation(
                    p_sb[:, 0:T], e2[:], AF.Exp, accum_out=zd),
                    waits=[t_diff, last_ptr])
                pend = (q, {"t_pd": t_pd})
        wmB = b.wm()

        tnb = None
        for j in range(NDS):
            tk = b.op("T", lambda t_, j=j: t_.matmul(
                psA[:, (j % 4) * 512:(j % 4) * 512 + DS], ones_col[:],
                ns_row[0:1, j * DS:(j + 1) * DS], start=True, stop=True),
                waits=wmB + [bank_tok.get((id(psA), j % 4))])
            tnb = b.op("V", lambda v, j=j: v.tensor_copy(
                ns_b[:, j * DS:(j + 1) * DS],
                psA[:, (j % 4) * 512:(j % 4) * 512 + DS]), waits=[tk])
            bank_free(psA, (j % 4) * 512, DS, tnb)
        sumsq = stats[:, 14:15]
        rstd2 = stats[:, 15:16]
        c_ev = None
        t_t1 = None
        for q in range(NQ):
            b.op("S", lambda s, q=q: s.activation(
                junk_bf[:], out_sb[:, q * D:(q + 1) * D], AF.Square,
                accum_out=sumsq), waits=wmB + [t_t1])
            b.op("S", lambda s: s.activation(
                rstd2, sumsq, AF.Ln, scale=1.0 / D, bias=eps_rms))
            t_b_ = b.op("S", lambda s: s.activation(rstd2, rstd2, AF.Exp,
                                                    scale=-0.5))
            b.op("V", lambda v: v.tensor_mul(rstd2, rstd2, one_minus),
                 waits=[t_b_])
            t_t1 = b.op("V", lambda v, q=q: v.tensor_scalar(
                out=tmp_f[:], in0=out_sb[:, q * D:(q + 1) * D],
                scalar1=rstd2, scalar2=None, op0=ALU.mult))
            t_on = b.op("V", lambda v, q=q: v.tensor_mul(
                out_sb[:, q * D:(q + 1) * D], tmp_f[:], ns_b[:]),
                waits=[tnb])
            ttr = None
            for cc in range(NCH):
                ttr = b.op("T", lambda t_, cc=cc, q=q: t_.transpose(
                    psum_bf(psB, 0, 2048)[:, cc * 128:(cc + 1) * 128],
                    out_sb[:, q * D + cc * 128: q * D + (cc + 1) * 128],
                    ident_bf[:]),
                    waits=([t_on, c_ev, bank_tok.get((id(psB), 0)),
                            bank_tok.get((id(psB), 1))] if cc == 0 else []),
                    inc=(cc == NCH - 1))
            onv = out_nT[:].rearrange("p (c t) -> p c t", t=OWN)
            pv = psum_bf(psB, 0, 2048)[:].rearrange("p (c t) -> p c t", t=128)
            c_ev = b.op("V", lambda v, q=q, onv=onv, pv=pv: v.tensor_copy(
                onv[:, :, q * 128:(q + 1) * 128], pv[:, 0:NCH, :]),
                waits=[ttr])
        bank_free(psB, 0, 1024, c_ev)
        wmC1 = b.wm()
        onv = out_nT[:].rearrange("p (c t) -> p c t", t=OWN)
        t_word = {}
        for dsl in range(NDS):
            sl = dsl % 2
            wo_src = Wo.rearrange("(c p) n -> p c n", p=128)[:, :, dsl * DS:(dsl + 1) * DS]
            half = max(1, NCH // 2)
            war = [t_word.get(dsl - 2)] + (wmC1 if dsl < 2 else [])
            twd = None
            for piece in range(0, NCH, half):
                pe = min(piece + half, NCH)
                twd = b.dma("G", f"wo{sl}", wosl[sl][:, piece:pe, :],
                            wo_src[:, piece:pe, :], waits=war)
                war = []
            for q in range(NQ):
                n_lin = dsl * NQ + q
                xslq = xsl2[n_lin % 2]
                t_x = b.dma("Y", f"xc{n_lin % 2}", xslq[:],
                            xo[q * 128:(q + 1) * 128, dsl * DS:(dsl + 1) * DS],
                            waits=[t_xread.get(("C", n_lin - 2))]
                            + (wmC1 if n_lin < 2 else []))
                pc = (q % 4) * 512
                tk = mmgroup(psA, pc, DS,
                             [onv[:, cc, q * 128:(q + 1) * 128]
                              for cc in range(NCH)],
                             [wosl[sl][:, cc, :] for cc in range(NCH)],
                             waits=[twd] + (wmC1 if n_lin == 0 else []))
                t_add = b.op("V", lambda v, q=q, dsl=dsl, pc=pc, xslq=xslq:
                             v.scalar_tensor_tensor(
                                 out=resid[:, q * D + dsl * DS:
                                           q * D + (dsl + 1) * DS],
                                 in0=psA[:, pc:pc + DS], scalar=1.0,
                                 in1=xslq[:], op0=ALU.mult, op1=ALU.add),
                             waits=[tk, t_x])
                bank_free(psA, pc, DS, t_add)
                t_xread[("C", n_lin)] = t_add
            t_word[dsl] = b.tok("T")
        wmC = b.wm()

        d_tr_ev = None
        for q in range(NQ):
            xs = resid[:, q * D:(q + 1) * D]
            b.op("V", lambda v, xs=xs: v.reduce_sum(
                negm, xs, axis=AX.X, negate=True), waits=wmC)
            b.op("V", lambda v: v.tensor_scalar_mul(negm, negm, 1.0 / D))
            t_sq = b.op("S", lambda s, xs=xs: s.activation(
                junk_d[:], xs, AF.Square, accum_out=sq), waits=wmC)
            b.op("V", lambda v: v.tensor_mul(m2c, negm, negm))
            b.op("V", lambda v: v.tensor_scalar(
                out=var, in0=sq, scalar1=1.0 / D, scalar2=None, op0=ALU.mult),
                waits=[t_sq])
            t_v3 = b.op("V", lambda v: v.tensor_sub(var, var, m2c))
            b.op("S", lambda s: s.activation(rstd, var, AF.Ln, bias=eps_ln),
                 waits=[t_v3])
            t_r2 = b.op("S", lambda s: s.activation(rstd, rstd, AF.Exp,
                                                    scale=-0.5))
            t_ln = b.op("V", lambda v, xs=xs: v.tensor_scalar(
                out=lnscr[:], in0=xs, scalar1=negm, scalar2=rstd,
                op0=ALU.add, op1=ALU.mult), waits=[t_r2])
            ttr = None
            for cc in range(NCH):
                ttr = b.op("T", lambda t_, cc=cc: t_.transpose(
                    psum_bf(psA, 0, 2048)[:, cc * 128:(cc + 1) * 128],
                    lnscr[:, cc * 128:(cc + 1) * 128], ident_bf[:]),
                    waits=([t_ln, d_tr_ev, bank_tok.get((id(psA), 0)),
                            bank_tok.get((id(psA), 1))] if cc == 0 else []),
                    inc=(cc == NCH - 1))
            l2v = ln2_xT[:].rearrange("p (c t) -> p c t", t=OWN)
            pv = psum_bf(psA, 0, 2048)[:].rearrange("p (c t) -> p c t", t=128)
            d_tr_ev = b.op("V", lambda v, q=q, l2v=l2v, pv=pv: v.tensor_copy(
                l2v[:, :, q * 128:(q + 1) * 128], pv[:, 0:NCH, :]),
                waits=[ttr])
        bank_free(psA, 0, 1024, d_tr_ev)
        l2v = ln2_xT[:].rearrange("p (c t) -> p c t", t=OWN)
        nm1g = stats[:, 16:17]
        t4s = stats[:, 17:18]
        z4 = stats[:, 18:19]
        rz4 = stats[:, 19:20]
        top8 = stats[:, 24:32]
        for q in range(NQ):
            tk = mmgroup(psA, 0, E,
                         [l2v[:, cc, q * 128:(q + 1) * 128]
                          for cc in range(NCH)] + [ones_col[:]],
                         [gwT[:, cc * 32:cc * 32 + E] for cc in range(NCH)]
                         + [gb_row[:]],
                         waits=[d_tr_ev])
            b.op("V", lambda v: v.max(out=top8, in_=psA[:, 0:E]), waits=[tk])
            b.op("V", lambda v: v.max_index(idx8[:], top8[:], psA[:, 0:E]))
            b.op("V", lambda v, q=q: v.tensor_copy(
                topidx[:, q * TOPK:(q + 1) * TOPK], idx8[:, 0:TOPK]))
            t_n1 = b.op("V", lambda v: v.tensor_scalar_mul(
                nm1g, top8[:, 0:1], -1.0))
            t_t4 = b.op("S", lambda s: s.activation(
                t4s, top8[:, TOPK - 1:TOPK], AF.Exp, bias=nm1g), waits=[t_n1])
            t_eL = b.op("S", lambda s: s.activation(
                eL_sb[:, 0:E], psA[:, 0:E], AF.Exp, bias=nm1g))
            b.op("V", lambda v, q=q: v.scalar_tensor_tensor(
                out=w_sb[:, q * E:(q + 1) * E], in0=eL_sb[:, 0:E], scalar=t4s,
                in1=eL_sb[:, 0:E], op0=ALU.is_ge, op1=ALU.mult,
                accum_out=z4), waits=[t_t4, t_eL])
            b.op("V", lambda v: v.reciprocal(rz4, z4))
            t_ws = b.op("V", lambda v, q=q: v.tensor_scalar_mul(
                w_sb[:, q * E:(q + 1) * E], w_sb[:, q * E:(q + 1) * E], rz4))
            bank_free(psA, 0, 512, t_eL)
            t_wt = b.op("T", lambda t_, q=q: t_.transpose(
                psA[0:E, 512:512 + 128], w_sb[:, q * E:(q + 1) * E],
                ident_f[:]), waits=[t_ws, bank_tok.get((id(psA), 1))])
            t_wtv = b.op("V", lambda v, q=q: v.tensor_copy(
                wT_sb[0:E, q * 128:(q + 1) * 128], psA[0:E, 512:512 + 128]),
                waits=[t_wt])
            bank_free(psA, 512, 512, t_wtv)
            for j in range(NDS):
                pc = 1024 + (j % 2) * 512
                tk2 = b.op("T", lambda t_, q=q, j=j, pc=pc: t_.matmul(
                    psA[:, pc:pc + DS], wT_sb[0:E, q * 128:(q + 1) * 128],
                    bd_all[0:E, j * DS:(j + 1) * DS], start=True, stop=True),
                    waits=[t_wtv, bank_tok.get((id(psA), pc // 512))])
                ta = b.op("V", lambda v, q=q, j=j, pc=pc: v.tensor_add(
                    resid[:, q * D + j * DS: q * D + (j + 1) * DS],
                    resid[:, q * D + j * DS: q * D + (j + 1) * DS],
                    psA[:, pc:pc + DS]), waits=[tk2])
                bank_free(psA, pc, DS, ta)
        t_gate_done = b.tok("V")

        if b.active == "G":
            bc_x_reg = b.nc.gpsimd.alloc_register("bc_x")
            bc_y_reg = b.nc.gpsimd.alloc_register("bc_y")
            b.nc.gpsimd.reg_mov(bc_x_reg, CAPTOT - 1)
            b.nc.gpsimd.reg_mov(bc_y_reg, OWN - 1)
        else:
            bc_x_reg = bc_y_reg = None

        sl_t = stats[:, 33:34]
        g_t = stats[:, 34:35]
        ovf_t = stats[:, 35:36]
        t_mk = b.op("V", lambda v: v.tensor_scalar(
            out=maskT[:], in0=wT_sb[0:E, 0:OWN], scalar1=0.0, scalar2=None,
            op0=ALU.is_gt), waits=[t_gate_done])
        t_sc = b.op("V", lambda v: v.tensor_tensor_scan(
            out=posT[:], data0=maskT[:], data1=maskT[:], initial=0.0,
            op0=ALU.add, op1=ALU.bypass))
        t_idx_done = None
        for tt in range(NQ):
            t_tr = b.op("T", lambda t_, tt=tt: t_.transpose(
                psA[0:128, 0:E], posT[0:E, tt * 128:(tt + 1) * 128],
                ident_f[0:E, 0:E]),
                waits=[t_sc, bank_tok.get((id(psA), 0))])
            t_ptk = b.op("V", lambda v: v.tensor_copy(eL_sb[:], psA[:, 0:E]),
                         waits=[t_tr])
            bank_free(psA, 0, 512, t_ptk)
            for j in range(TOPK):
                col = tt * TOPK + j
                ej = topidx[:, col:col + 1]
                b.op("V", lambda v, ej=ej: v.tensor_scalar(
                    out=gcl[:, 0:E], in0=iota_e[:], scalar1=ej, scalar2=None,
                    op0=ALU.is_equal))
                b.op("V", lambda v: v.tensor_mul(gcl[:, 0:E], gcl[:, 0:E],
                                                 eL_sb[:]))
                b.op("V", lambda v: v.reduce_sum(sl_t, gcl[:, 0:E], axis=AX.X))
                b.op("V", lambda v, ej=ej: v.scalar_tensor_tensor(
                    out=g_t, in0=ej, scalar=float(CAP), in1=sl_t,
                    op0=ALU.mult, op1=ALU.add))
                b.op("V", lambda v: v.tensor_scalar_add(g_t, g_t, -1.0))
                b.op("V", lambda v: v.tensor_scalar(
                    out=ovf_t, in0=sl_t, scalar1=float(CAP + 1), scalar2=None,
                    op0=ALU.is_ge))
                b.op("V", lambda v: v.scalar_tensor_tensor(
                    out=g_t, in0=ovf_t, scalar=2.0e9, in1=g_t,
                    op0=ALU.mult, op1=ALU.add))
                t_idx_done = b.op("V", lambda v, col=col: v.tensor_copy(
                    idx_sb[:, col:col + 1], g_t))

        wv_view = lnscr2[:, D:D + 2 * E].bitcast(F32)
        ti_view = lnscr2[:, D + 2 * E:D + 2 * E + 2].bitcast(mybir.dt.int32)
        b.op("V", lambda v: v.memset(lnscr2[:, D + 2 * E + 2:RW], 0.0),
             waits=wmC)
        t_sca = None
        for tt in range(NQ):
            ttr = None
            for cc in range(NCH):
                ttr = b.op("T", lambda t_, cc=cc, tt=tt: t_.transpose(
                    psum_bf(psA, 0, 2048)[:, cc * 128:(cc + 1) * 128],
                    ln2_xT[:, cc * OWN + tt * 128: cc * OWN + (tt + 1) * 128],
                    ident_bf[:]),
                    waits=([t_idx_done, bank_tok.get((id(psA), 0)),
                            bank_tok.get((id(psA), 1))] if cc == 0 else []),
                    inc=(cc == NCH - 1))
            pv = psum_bf(psA, 0, 2048)[:].rearrange("p (c t) -> p c t", t=128)
            ln2v = lnscr2[:, 0:D].rearrange("p (c t) -> p c t", t=128)
            t_ev = b.op("V", lambda v, pv=pv, ln2v=ln2v: v.tensor_copy(
                ln2v, pv[:, 0:NCH, :]),
                waits=[ttr, t_sca])
            bank_free(psA, 0, 1024, t_ev)
            b.op("V", lambda v, tt=tt: v.tensor_copy(
                wv_view, w_sb[:, tt * E:(tt + 1) * E]))
            t_row = b.op("V", lambda v, tt=tt: v.tensor_copy(
                ti_view, tokid_t[:, tt:tt + 1]))
            for j in range(TOPK):
                col = tt * TOPK + j
                t_sca = b.dma("G", "sca", None, None) if False else None
                if b.active == "G":
                    for tk_ in [t_row, t_pfx]:
                        b.wait(("G"), tk_)
                b.dmacnt["sca"] += 16
                if b.active == "G":
                    b.nc.gpsimd.indirect_dma_start(
                        out=xarena[:],
                        out_offset=bass.IndirectOffsetOnAxis(
                            ap=idx_sb[:, col:col + 1], axis=0),
                        in_=lnscr2[:],
                        in_offset=None,
                        bounds_check=bc_x_reg,
                        oob_is_err=False,
                    ).then_inc(env["dsems"]["sca"], 16)
                t_sca = (env["dsems"]["sca"], b.dmacnt["sca"])
        t_scat_all = t_sca

        gu_read = {}
        wd_read = {}

        def issue_gu(k, first_waits=()):
            e_, i_ = divmod(k, NI)
            if e_ >= E:
                return None
            sl = k % 3
            wgp, wup = gusl[sl]
            src_g = wg_h[e_].rearrange("(c p) n -> p c n", p=128)[:, :, i_ * 128:(i_ + 1) * 128]
            src_u = wu_h[e_].rearrange("(c p) n -> p c n", p=128)[:, :, i_ * 128:(i_ + 1) * 128]
            war = list(first_waits) + [gu_read.get(k - 3)]
            half = max(1, NCH // 2)
            tk = None
            for piece in range(0, NCH, half):
                pe_ = min(piece + half, NCH)
                b.dma("G", f"gu{sl}", wgp[:, piece:pe_, :],
                      src_g[:, piece:pe_, :], waits=war)
                war = []
                tk = b.dma("G", f"gu{sl}", wup[:, piece:pe_, :],
                           src_u[:, piece:pe_, :])
            return tk

        def issue_wd(k, first_waits=()):
            e_, j_ = divmod(k, NDS)
            if e_ >= E:
                return None
            sl = k % 2
            src_d = wd_h[e_].rearrange("(c p) n -> p c n", p=128)[:, :, j_ * DS:(j_ + 1) * DS]
            return b.dma("G", f"wd{sl}", wdsl[sl][:], src_d,
                         waits=list(first_waits) + [wd_read.get(k - 2)])

        gu_tok = {}
        wd_tok = {}
        for k in range(3):
            gu_tok[k] = issue_gu(k, first_waits=wmC + [t_scat_all])
        for k in range(2):
            wd_tok[k] = issue_wd(k, first_waits=wmC)

        xav2 = xarena.rearrange("(e t p) c -> e p t c", p=128, t=NCT)
        xtv = xtT[:].rearrange("p (c s) -> p c s", s=CAP)
        t_xt_done = {}
        t_gu_mm = {}
        t_y_read = {}
        ya_tok = {}
        t_wtib = {}
        ysc_n = [0]
        t_act_read = {}
        for e in range(E):
            t_xl = b.dma("Y", "xld", xt_sb[:], xav2[e],
                         waits=[t_scat_all, t_xt_done.get(e - 1)]
                         + (t_wtib.get(e - 1) or []))
            for tile in range(NCT):
                ttr = None
                for cc in range(NCH):
                    ttr = b.op("T", lambda t_, cc=cc, tile=tile: t_.transpose(
                        psum_bf(psA, 0, 2048)[:, cc * 128:(cc + 1) * 128],
                        xt_sb[:, tile, cc * 128:(cc + 1) * 128], ident_bf[:]),
                        waits=([t_xl, bank_tok.get((id(psA), 0)),
                                bank_tok.get((id(psA), 1))] if cc == 0 else []),
                        inc=(cc == NCH - 1))
                pv = psum_bf(psA, 0, 2048)[:].rearrange("p (c t) -> p c t",
                                                        t=128)
                tev = b.op("V", lambda v, pv=pv, tile=tile: v.tensor_copy(
                    xtv[:, :, tile * 128:(tile + 1) * 128], pv[:, 0:NCH, :]),
                    waits=[ttr, t_gu_mm.get(e - 1)])
                bank_free(psA, 0, 1024, tev)
            t_xt_done[e] = b.tok("T")
            for tile in range(NCT):
                b.op("V", lambda v, tile=tile: v.tensor_copy(
                    wbuf[:, tile:tile + 1],
                    xt_sb[:, tile, D + 2 * e:D + 2 * e + 2].bitcast(F32)),
                    waits=[t_xl] + (ya_tok.get(e - 1) or []))
                b.op("V", lambda v, tile=tile: v.tensor_copy(
                    tibuf[:, tile:tile + 1],
                    xt_sb[:, tile, D + 2 * E:D + 2 * E + 2].bitcast(
                        mybir.dt.int32)))
            t_wtib[e] = [b.tok("V")]
            t_xtT_ready = b.tok("V")
            for i in range(NI):
                k = e * NI + i
                sl = k % 3
                wgp, wup = gusl[sl]
                par = i % 2
                colg, colu = (0, 512) if par == 0 else (1024, 1536)
                tg_ = mmgroup(psA, colg, CAP if CAP <= 512 else 512,
                              [wgp[:, cc, :] for cc in range(NCH)],
                              [xtv[:, cc, 0:CAP] for cc in range(NCH)],
                              waits=[gu_tok.get(k), t_xtT_ready])
                tu_ = mmgroup(psA, colu, CAP,
                              [wup[:, cc, :] for cc in range(NCH)],
                              [xtv[:, cc, 0:CAP] for cc in range(NCH)])
                bcg = bgT_all[:, i * E + e:i * E + e + 1]
                bcu = buT_all[:, i * E + e:i * E + e + 1]
                t1 = b.op("V", lambda v, colg=colg, bcg=bcg: v.tensor_scalar(
                    out=gcl[:], in0=psA[:, colg:colg + CAP], scalar1=bcg,
                    scalar2=7.0, op0=ALU.add, op1=ALU.min), waits=[tg_])
                t2 = b.op("S", lambda s: s.activation(
                    sig[:], gcl[:], AF.Sigmoid, scale=1.702), waits=[t1])
                b.op("V", lambda v, colu=colu, bcu=bcu: v.tensor_scalar(
                    out=ucl[:], in0=psA[:, colu:colu + CAP], scalar1=bcu,
                    scalar2=7.0, op0=ALU.add, op1=ALU.min), waits=[tu_])
                t4_ = b.op("V", lambda v: v.tensor_scalar_max(
                    ucl[:], ucl[:], -7.0))
                t5 = b.op("V", lambda v: v.tensor_mul(gcl[:], gcl[:], sig[:]),
                          waits=[t2])
                b.op("V", lambda v, i=i: v.scalar_tensor_tensor(
                    out=actT[:, i * CAP:(i + 1) * CAP],
                    in0=ucl[:], scalar=1.0, in1=gcl[:],
                    op0=ALU.add, op1=ALU.mult),
                    waits=[t_act_read.get(e - 1)])
                bank_free(psA, colg, CAP, t5)
                bank_free(psA, colu, CAP, t4_)
                gu_read[k] = b.tok("T")
                gu_tok[k + 3] = issue_gu(k + 3)
            t_gu_mm[e] = b.tok("T")
            t_act_ready = b.tok("V")
            yr_toks = []
            t_ev_tile = [None] * NCT
            for j in range(NDS):
                k = e * NDS + j
                tk_d = wd_tok.get(k)
                sl = k % 2
                for tile in range(NCT):
                    pc = (tile % 4) * 512
                    tk = mmgroup(psB, pc, DS,
                                 [actT[:, i2 * CAP + tile * 128:
                                       i2 * CAP + (tile + 1) * 128]
                                  for i2 in range(NI)],
                                 [wdsl[sl][:, i2, :] for i2 in range(NI)],
                                 waits=[tk_d, t_act_ready])
                    yb = ysbt[tile]
                    wcol = wbuf[:, tile:tile + 1]
                    par = tile % 2
                    tev = b.op("V", lambda v, pc=pc, yb=yb, wcol=wcol, j=j:
                               v.tensor_scalar(
                                   out=yb[:, j * DS:(j + 1) * DS],
                                   in0=psB[:, pc:pc + DS],
                                   scalar1=wcol, scalar2=None, op0=ALU.mult),
                               waits=[tk]
                               + ([(env["dsems"][f"ya{par}"],
                                    b.dmacnt[f"ya{par}"])] if j == 0 else []))
                    bank_free(psB, pc, DS, tev)
                    yr_toks.append(tev)
                    t_ev_tile[tile] = tev
                wd_read[k] = b.tok("T")
                wd_tok[k + 2] = issue_wd(k + 2)
            for tile in range(NCT):
                ti_v = tibuf[:, tile:tile + 1]
                par = tile % 2
                if b.active == "G":
                    b.wait("G", t_ev_tile[tile])
                    b.wait("G", t_pfy)
                    b.wait("G", (env["dsems"]["ya0"], b.dmacnt["ya0"]))
                    b.wait("G", (env["dsems"]["ya1"], b.dmacnt["ya1"]))
                b.dmacnt[f"ya{par}"] += 16
                if b.active == "G":
                    b.nc.gpsimd.indirect_dma_start(
                        out=yarena,
                        out_offset=bass.IndirectOffsetOnAxis(
                            ap=ti_v, axis=0),
                        in_=ysbt[tile][:],
                        in_offset=None,
                        bounds_check=bc_y_reg,
                        oob_is_err=False,
                        compute_op=ALU.add,
                    ).then_inc(env["dsems"][f"ya{par}"], 16)
            t_act_read[e] = b.tok("T")
            t_y_read[e] = yr_toks[-2:]
            ya_tok[e] = [(env["dsems"]["ya0"], b.dmacnt["ya0"]),
                         (env["dsems"]["ya1"], b.dmacnt["ya1"])]
        t_ya_all = [(env["dsems"]["ya0"], b.dmacnt["ya0"]),
                    (env["dsems"]["ya1"], b.dmacnt["ya1"])]

        t_cv = None
        for tt in range(NQ):
            t_yl = b.dma("Y", "yrd", ysbt[0][:], yav[tt],
                         waits=t_ya_all + [t_cv])
            t_cv = b.op("V", lambda v: v.tensor_copy(tmp_f[:], ysbt[0][:]),
                        waits=[t_yl])
            b.op("V", lambda v, tt=tt: v.tensor_add(
                resid[:, tt * D:(tt + 1) * D], resid[:, tt * D:(tt + 1) * D],
                tmp_f[:]))

        wmE = b.wm()
        st_tok = None
        for q in range(NQ):
            st_tok = b.dma("Y", "st", y_h[q * 128:(q + 1) * 128, :],
                           resid[:, q * D:(q + 1) * D], waits=wmE)
        if b.active == "Y":
            b.nc.sync.wait_ge(env["dsems"]["st"], st_tok[1])

    with es_glob:
        with nc.Block() as blk:
            @blk.tensor
            def _(t):
                walk(EB(nc, "T", env))

            @blk.vector
            def _(v):
                walk(EB(nc, "V", env))

            @blk.scalar
            def _(s):
                walk(EB(nc, "S", env))

            @blk.gpsimd
            def _(g):
                walk(EB(nc, "G", env))

            @blk.sync
            def _(y):
                walk(EB(nc, "Y", env))
    return nc


def kernel(L0_attn_x, L0_attn_Wq, L0_attn_Wk, L0_attn_Wv, L0_attn_Wo,
           L0_attn_norm_scale, L0_attn_lam, L0_ffn_gate_w, L0_ffn_gate_b,
           L0_ffn_w_gate, L0_ffn_b_gate, L0_ffn_w_up, L0_ffn_b_up,
           L0_ffn_w_down, L0_ffn_b_down):
    x = np.ascontiguousarray(np.asarray(L0_attn_x, dtype=np.float32))
    B, S, Dm = x.shape
    nc = build_program(CFG_FULL)
    f32c = lambda a: np.ascontiguousarray(np.asarray(a, dtype=np.float32))
    shared = {
        "Wq": f32c(L0_attn_Wq), "Wk": f32c(L0_attn_Wk), "Wv": f32c(L0_attn_Wv),
        "Wo": f32c(L0_attn_Wo), "norm_scale": f32c(L0_attn_norm_scale),
        "lam": f32c(L0_attn_lam), "gate_w": f32c(L0_ffn_gate_w),
        "gate_b": f32c(L0_ffn_gate_b), "w_gate": f32c(L0_ffn_w_gate),
        "b_gate": f32c(L0_ffn_b_gate), "w_up": f32c(L0_ffn_w_up),
        "b_up": f32c(L0_ffn_b_up), "w_down": f32c(L0_ffn_w_down),
        "b_down": f32c(L0_ffn_b_down),
    }
    in_maps = []
    half = S // 2
    for core in range(8):
        bidx, h = core // 2, core % 2
        m = dict(shared)
        m["x_own"] = np.ascontiguousarray(x[bidx, h * half:(h + 1) * half])
        m["x_part"] = np.ascontiguousarray(x[bidx, (1 - h) * half:(2 - h) * half])
        in_maps.append(m)
    res = run_bass_kernel_spmd(nc, in_maps, core_ids=list(range(8)))
    y = np.empty((B, S, Dm), dtype=np.float32)
    for core in range(8):
        bidx, h = core // 2, core % 2
        y[bidx, h * half:(h + 1) * half] = res.results[core]["y_own"]
    return y


# version 21
# speedup vs baseline: 1.2078x; 1.0933x over previous
import contextlib
import numpy as np
import ml_dtypes

import concourse.bass as bass
import concourse.mybir as mybir
from concourse.bass_utils import run_bass_kernel_spmd

F32 = mybir.dt.float32
BF16 = mybir.dt.bfloat16
AF = mybir.ActivationFunctionType
ALU = mybir.AluOpType
AX = mybir.AxisListType

SBUF_BASE = 16512

CFG_FULL = dict(d_model=2048, own=1024, part=1024, heads=16, experts=32,
                topk=4, dff=1024)

DETECT_RACES = True


def derive(cfg):
    c = dict(cfg)
    c["NCH"] = cfg["d_model"] // 128
    c["NQ"] = cfg["own"] // 128
    c["NT"] = (cfg["own"] + cfg["part"]) // 128
    c["T"] = cfg["own"] + cfg["part"]
    c["NI"] = cfg["dff"] // 128
    c["ST"] = min(512, cfg["own"])
    c["NST"] = cfg["own"] // c["ST"]
    c["DS"] = min(512, cfg["d_model"])
    c["NDS"] = cfg["d_model"] // c["DS"]
    c["QS"] = 1.0 / float(np.sqrt(np.float32(128)))
    mean_cnt = cfg["own"] * cfg["topk"] // cfg["experts"]
    c["CAP"] = min(384, max(128, ((3 * mean_cnt + 127) // 128) * 128))
    c["NCT"] = c["CAP"] // 128
    c["RW"] = ((cfg["d_model"] + 2 * cfg["experts"] + 2 + 15) // 16) * 16
    return c


class EB:

    ENGS = ("T", "V", "S", "G", "Y")

    def __init__(self, nc, active, env):
        self.nc = nc
        self.active = active
        self.env = env
        self.cnt = {e: 0 for e in self.ENGS}
        self.dmacnt = dict.fromkeys(env["dsems"], 0)
        self.waited = {}

    def _eng(self, e):
        nc = self.nc
        return {"T": nc.tensor, "V": nc.vector, "S": nc.scalar,
                "G": nc.gpsimd, "Y": nc.sync}[e]

    def wait(self, e, tok):
        if tok is None or self.active != e:
            return
        sem, val = tok
        if val <= 0:
            return
        key = id(sem)
        if self.waited.get(key, -1) < val:
            self._eng(e).wait_ge(sem, val)
            self.waited[key] = val

    def op(self, e, fn, waits=(), inc=True):
        if self.active == e:
            for tk in waits:
                self.wait(e, tk)
            if e in ("V", "S", "G") and self.cnt[e] > 0:
                self.wait(e, (self.env["sems"][e], self.cnt[e]))
            ins = fn(self._eng(e))
            if inc:
                ins.then_inc(self.env["sems"][e], 1)
        if inc:
            self.cnt[e] += 1
            return (self.env["sems"][e], self.cnt[e])
        return None

    def tok(self, e):
        return (self.env["sems"][e], self.cnt[e])

    def dma(self, q, sem_name, out, in_, waits=()):
        sem = self.env["dsems"][sem_name]
        self.dmacnt[sem_name] += 16
        if self.active == q:
            for tk in waits:
                self.wait(q, tk)
            self._eng(q).dma_start(out=out, in_=in_).then_inc(sem, 16)
        return (sem, self.dmacnt[sem_name])

    def wm(self):
        return [(self.env["sems"][e], self.cnt[e]) for e in ("T", "V", "S")]


def build_program(cfg):
    c = derive(cfg)
    D, H, E, I = cfg["d_model"], cfg["heads"], cfg["experts"], cfg["dff"]
    NCH, NQ, NT, NI = c["NCH"], c["NQ"], c["NT"], c["NI"]
    ST, NST, DS, NDS = c["ST"], c["NST"], c["DS"], c["NDS"]
    T = c["T"]
    OWN = cfg["own"]
    TOPK = cfg["topk"]
    CAP, NCT, RW = c["CAP"], c["NCT"], c["RW"]
    CAPTOT = E * CAP

    nc = bass.Bass("TRN2", target_bir_lowering=False,
                   detect_race_conditions=DETECT_RACES)

    xo = nc.dram_tensor("x_own", [OWN, D], F32, kind="ExternalInput").ap()
    xp = nc.dram_tensor("x_part", [cfg["part"], D], F32, kind="ExternalInput").ap()
    Wq = nc.dram_tensor("Wq", [D, H * 256], F32, kind="ExternalInput").ap()
    Wk = nc.dram_tensor("Wk", [D, H * 256], F32, kind="ExternalInput").ap()
    Wv = nc.dram_tensor("Wv", [D, H * 128], F32, kind="ExternalInput").ap()
    Wo = nc.dram_tensor("Wo", [H * 128, D], F32, kind="ExternalInput").ap()
    ns_h = nc.dram_tensor("norm_scale", [D], F32, kind="ExternalInput").ap()
    lam_h = nc.dram_tensor("lam", [H], F32, kind="ExternalInput").ap()
    gw_h = nc.dram_tensor("gate_w", [E, D], F32, kind="ExternalInput").ap()
    gb_h = nc.dram_tensor("gate_b", [E], F32, kind="ExternalInput").ap()
    wg_h = nc.dram_tensor("w_gate", [E, D, I], F32, kind="ExternalInput").ap()
    bg_h = nc.dram_tensor("b_gate", [E, I], F32, kind="ExternalInput").ap()
    wu_h = nc.dram_tensor("w_up", [E, D, I], F32, kind="ExternalInput").ap()
    bu_h = nc.dram_tensor("b_up", [E, I], F32, kind="ExternalInput").ap()
    wd_h = nc.dram_tensor("w_down", [E, I, D], F32, kind="ExternalInput").ap()
    bd_h = nc.dram_tensor("b_down", [E, D], F32, kind="ExternalInput").ap()
    y_h = nc.dram_tensor("y_own", [OWN, D], F32, kind="ExternalOutput").ap()

    id_bf_h = nc.inline_tensor(
        np.eye(128, dtype=np.float32).astype(ml_dtypes.bfloat16), name="idbf").ap()
    id_f_h = nc.inline_tensor(np.eye(128, dtype=np.float32), name="idf").ap()
    iota_e_h = nc.inline_tensor(
        np.tile(np.arange(E, dtype=np.float32), (128, 1)), name="iotae").ap()
    tokid_h = nc.inline_tensor(
        (np.arange(NQ, dtype=np.int32)[None, :] * 128
         + np.arange(128, dtype=np.int32)[:, None]).copy(), name="tokid").ap()
    xarena = nc.dram_tensor("xarena", [CAPTOT, RW], BF16, kind="Internal").ap()
    yarena = nc.dram_tensor("yarena", [OWN, D], BF16, kind="Internal").ap()

    def at(name, shape, dt, off):
        return nc.alloc_sbuf_tensor_at(
            name, shape, dt, offset=SBUF_BASE + off).ap()

    KB = 1024
    A32 = lambda x: (x + 31) & ~31
    cur = [0]

    def nxt(nbytes):
        off = A32(cur[0])
        cur[0] = off + nbytes
        return off

    ident_bf = at("ident_bf", [128, 128], BF16, nxt(256))
    ident_f = at("ident_f", [128, 128], F32, nxt(512))
    ones_col = at("ones_col", [1, 128], F32, nxt(512))
    ones_row = at("ones_row", [1, 512], F32, nxt(2048))
    lam_b = at("lam_b", [128, H], F32, nxt(H * 4))
    stats = at("stats", [128, 64], F32, nxt(256))
    gwT = at("gwT", [128, NCH * 32], BF16, nxt(NCH * 64))
    gb_row = at("gb_row", [1, E], F32, nxt(E * 4))
    bgT_all = at("bgT_all", [128, NI * E], F32, nxt(NI * E * 4))
    buT_all = at("buT_all", [128, NI * E], F32, nxt(NI * E * 4))
    bd_all = at("bd_all", [E, D], F32, nxt(D * 4))
    ns_row = at("ns_row", [1, D], F32, nxt(D * 4))
    iota_e = at("iota_e", [128, E], F32, nxt(E * 4))
    tokid_t = at("tokid_t", [128, NQ], mybir.dt.int32, nxt(NQ * 4))
    assert cur[0] <= 24 * KB, cur[0]

    R1, R2, R3, R4, R5 = 24 * KB, 88 * KB, 120 * KB, 160 * KB, 200 * KB
    ln_xT = at("ln_xT", [128, NCH * T], BF16, R1)
    resid = at("resid", [128, NQ * D], F32, R1)
    assert max(NCH * T * 2, NQ * D * 4) <= R2 - R1
    stage_f = at("stage_f", [E, max(I, D)], F32, R2)
    gw_sb = at("gw_sb", [E, D], BF16, R2 + max(I, D) * 4)
    out_sb = at("out_sb", [128, NQ * D], BF16, R2)
    ln2_xT = at("ln2_xT", [128, NCH * OWN], BF16, R2)
    assert NQ * D * 2 <= R3 - R2 and NCH * OWN * 2 <= R3 - R2
    e1 = at("e1", [128, T], F32, R3)
    e2 = at("e2", [128, T], F32, R3 + T * 4)
    o = R3 + 2 * T * 4
    k1T = at("k1T", [128, T], BF16, o); o += T * 2
    k2T = at("k2T", [128, T], BF16, o); o += T * 2
    v_sb = at("v_sb", [128, T], BF16, o); o += T * 2
    q1T = at("q1T", [128, OWN], BF16, o); o += OWN * 2
    q2T = at("q2T", [128, OWN], BF16, o); o += OWN * 2
    p_sb = at("p_sb", [128, T], BF16, o); o += T * 2
    pt_sb = at("pt_sb", [128, T], BF16, o); o += T * 2
    assert o <= R4, o
    out_nT = at("out_nT", [128, NCH * OWN], BF16, R3)
    ns_b = at("ns_b", [128, D], F32, R3 + NCH * OWN * 2)
    assert R3 + NCH * OWN * 2 + D * 4 <= R4
    actT = at("actT", [128, NI * CAP], BF16, R3)
    dof = R3 + NI * CAP * 2
    def nx2(n):
        nonlocal_holder[0] = (nonlocal_holder[0] + 31) & ~31
        off = nonlocal_holder[0]
        nonlocal_holder[0] = off + n
        return off
    nonlocal_holder = [dof]
    xtT = at("xtT", [128, NCH * CAP], BF16, nx2(NCH * CAP * 2))
    w_sb = at("w_sb", [128, NQ * E], F32, nx2(NQ * E * 4))
    lnscr = at("lnscr", [128, D], BF16, nx2(D * 2))
    wT_sb = at("wT_sb", [E, OWN], F32, nx2(OWN * 4))
    eL_sb = at("eL_sb", [128, E], F32, nx2(E * 4))
    gcl = at("gcl", [128, CAP], F32, nx2(CAP * 4))
    ucl = at("ucl", [128, CAP], F32, nx2(CAP * 4))
    sig = at("sig", [128, CAP], F32, nx2(CAP * 4))
    idx_sb = at("idx_sb", [128, NQ * TOPK], mybir.dt.int32, nx2(NQ * TOPK * 4))
    topidx = at("topidx", [128, NQ * TOPK], F32, nx2(NQ * TOPK * 4))
    idx8 = at("idx8", [128, 8], mybir.dt.uint32, nx2(32))
    wbuf = at("wbuf", [128, 4], F32, nx2(16))
    tibuf = at("tibuf", [128, 4], mybir.dt.int32, nx2(16))
    ysb_r3 = [at(f"ysbr{i}", [128, D], BF16, nx2(D * 2))
              for i in range(max(0, NCT - 2))]
    dof = nonlocal_holder[0]
    assert dof <= R4, dof
    xt_sb = at("xt_sb", [128, NCT, RW], BF16, R2)
    junk_d = at("junk_d", [128, D], BF16, R3)
    ysbt = []
    for i in range(min(2, NCT)):
        ysbt.append(at(f"ysb{i}", [128, D], BF16,
                       R2 + NCT * RW * 2 + i * D * 2))
    assert NCT * RW * 2 + min(2, NCT) * D * 2 <= R3 - R2
    ysbt.extend(ysb_r3)
    posT = at("posT", [E, OWN], F32, R5)
    maskT = at("maskT", [E, OWN], F32, R4 + 36 * KB)
    assert R5 + OWN * 4 <= 212832
    zt = at("zt", [128, RW], BF16, R3)
    lnscr2 = at("lnscr2", [128, RW], BF16, R4)
    xsl = [at(f"xsl{i}", [128, D], F32, R4 + i * D * 4) for i in range(2)]
    WQKB = NCH * 1024 + NCH * 256
    wqkv = []
    for sl in range(2):
        base = R4 + sl * WQKB
        wqkv.append((at(f"wq{sl}", [128, NCH, 256], BF16, base),
                     at(f"wk{sl}", [128, NCH, 256], BF16, base + NCH * 512),
                     at(f"wv{sl}", [128, NCH, 128], BF16, base + NCH * 1024)))
    assert 2 * WQKB <= 40 * KB
    wosl = [at(f"wo{sl}", [128, NCH, DS], BF16, R4 + sl * NCH * DS * 2)
            for sl in range(2)]
    assert 2 * NCH * DS * 2 <= 40 * KB
    gusl = [(at(f"wgp{sl}", [128, NCH, 128], BF16, R4 + sl * NCH * 512),
             at(f"wup{sl}", [128, NCH, 128], BF16,
                R4 + sl * NCH * 512 + NCH * 256)) for sl in range(3)]
    wdsl = [at(f"wdp{sl}", [128, NI, DS], BF16,
               R4 + 3 * NCH * 512 + sl * NI * DS * 2) for sl in range(2)]
    assert 3 * NCH * 512 + 2 * NI * DS * 2 <= 40 * KB
    junk_bf = at("junk_bf", [128, D], BF16, R4 + 32 * KB)
    tmp_f = at("tmp_f", [128, D], F32, R4 + 36 * KB)
    xsl2 = [at(f"xc{i}", [128, DS], F32, R5 + 2 * KB + i * DS * 4)
            for i in range(2)]
    assert R5 + 2 * KB + 2 * DS * 4 <= 212832

    es_glob = contextlib.ExitStack()
    psA = es_glob.enter_context(nc.psum_tensor("psA", [128, 2048], F32))
    psB = es_glob.enter_context(nc.psum_tensor("psB", [128, 2048], F32))

    sems = {e: es_glob.enter_context(nc.semaphore(f"sem{e}")) for e in EB.ENGS}
    dnames = (["cstY", "cstG", "st", "pfx", "pfy", "sca", "xld", "ya0", "ya1", "yrd"]
              + [f"xsl{i}" for i in range(2)]
              + [f"wqk{i}" for i in range(2)]
              + [f"wo{i}" for i in range(2)]
              + [f"gu{i}" for i in range(3)]
              + [f"wd{i}" for i in range(2)]
              + [f"xc{i}" for i in range(2)])
    dsems = {n: es_glob.enter_context(nc.semaphore(f"d_{n}")) for n in dnames}
    env = {"sems": sems, "dsems": dsems}

    EPS_LN = 1e-5
    EPS_RMS = 1e-6

    def psum_bf(ps, col0_f32, ncols_bf):
        return ps[:, col0_f32:col0_f32 + (ncols_bf + 1) // 2].bitcast(BF16)

    def chunks(apT, n_per, width):
        return apT[:].rearrange("p (c t) -> p c t", t=n_per)

    def walk(b: EB):
        bank_tok = {}

        def mmgroup(ps, col0, n, lhsTs, rhss, waits=()):
            tk = None
            nmm = len(lhsTs)
            for i, (lh, rh) in enumerate(zip(lhsTs, rhss)):
                w = []
                if i == 0:
                    w = list(waits)
                    for cb in range(col0 // 512, (col0 + n + 511) // 512):
                        w.append(bank_tok.get((id(ps), cb)))
                tk = b.op("T", lambda t_, lh=lh, rh=rh, i=i: t_.matmul(
                    ps[:, col0:col0 + n], lh, rh,
                    start=(i == 0), stop=(i == nmm - 1)),
                    waits=w, inc=(i == nmm - 1))
            return tk

        def bank_free(ps, col0, n, tok):
            for cb in range(col0 // 512, (col0 + n + 511) // 512):
                bank_tok[(id(ps), cb)] = tok

        b.dma("Y", "cstY", ident_bf[:], id_bf_h[:])
        b.dma("Y", "cstY", ident_f[:], id_f_h[:])
        b.dma("Y", "cstY", lam_b[0:1, :], lam_h[None, :])
        b.dma("Y", "cstY", ns_row[:], ns_h[None, :])
        b.dma("Y", "cstY", gb_row[:], gb_h[None, :])
        b.dma("Y", "cstY", bd_all[:], bd_h[:])
        t_gw = b.dma("G", "cstG", gw_sb[:], gw_h[:])
        b.dma("Y", "cstY", iota_e[:], iota_e_h[:])
        b.dma("Y", "cstY", tokid_t[:], tokid_h[:])
        t_bg = b.dma("Y", "cstY", stage_f[0:E, 0:I], bg_h[:])
        tok_cy = t_bg
        t_id1 = t_id2 = t_lam = tok_cy
        t_m1 = b.op("V", lambda v: v.memset(ones_col[:], 1.0))
        b.op("V", lambda v: v.memset(ones_row[:], 1.0))
        eps_ln = stats[:, 21:22]
        eps_rms = stats[:, 22:23]
        b.op("V", lambda v: v.memset(eps_ln, EPS_LN))
        b.op("V", lambda v: v.memset(eps_rms, EPS_RMS))
        b.op("V", lambda v: v.memset(zt[:], 0.0))
        t_zt = b.op("V", lambda v: v.memset(
            zt[:, D + 2 * E:D + 2 * E + 2].bitcast(mybir.dt.int32), OWN))
        tk = b.op("T", lambda t_: t_.matmul(psA[:, 0:H], ones_col[:],
                                            lam_b[0:1, :], start=True, stop=True),
                  waits=[t_lam, t_m1])
        t_lb = b.op("V", lambda v: v.tensor_copy(lam_b[:], psA[:, 0:H]),
                    waits=[tk])
        one_minus = stats[:, 20:21]
        b.op("V", lambda v: v.reduce_sum(one_minus, lam_b[:], axis=AX.X),
             waits=[t_lb])
        b.op("V", lambda v: v.tensor_scalar(
            out=one_minus, in0=one_minus, scalar1=-1.0 / H, scalar2=1.0,
            op0=ALU.mult, op1=ALU.add))
        gw_ev = []
        for cc in range(NCH):
            tt = b.op("T", lambda t_, cc=cc: t_.transpose(
                psum_bf(psB, (cc % 4) * 512, 2 * E)[:, 0:E],
                gw_sb[0:E, cc * 128:(cc + 1) * 128], ident_bf[0:E, 0:E]),
                waits=[t_gw, t_id1] + ([gw_ev[cc - 4]] if cc >= 4 else []))
            gw_ev.append(b.op("V", lambda v, cc=cc, tt=tt: v.tensor_copy(
                gwT[:, cc * 32:cc * 32 + E],
                psum_bf(psB, (cc % 4) * 512, 2 * E)[:, 0:E]), waits=[tt]))

        def bias_table(dst, t_loaded):
            toks = []
            for i in range(NI):
                col = 512 + (i % 2) * 512
                tt = b.op("T", lambda t_, i=i, col=col: t_.transpose(
                    psA[0:128, col:col + E],
                    stage_f[0:E, i * 128:(i + 1) * 128], ident_f[0:E, 0:E]),
                    waits=[t_loaded, t_id2] + ([toks[i - 2]] if i >= 2 else []))
                toks.append(b.op("V", lambda v, i=i, col=col, tt=tt: v.tensor_copy(
                    dst[:, i * E:(i + 1) * E], psA[:, col:col + E]), waits=[tt]))
            return toks
        tg_tab = bias_table(bgT_all, t_bg)
        t_bu = b.dma("Y", "cstY", stage_f[0:E, 0:I], bu_h[:],
                     waits=[tg_tab[-1]])
        tu_tab = bias_table(buT_all, t_bu)
        init_v = b.tok("V")
        bank_free(psA, 0, 2048, init_v)
        bank_free(psB, 0, 2048, init_v)

        negm = stats[:, 0:1]
        sq = stats[:, 1:2]
        var = stats[:, 2:3]
        m2c = stats[:, 3:4]
        rstd = stats[:, 4:5]
        t_xread = {}
        a_ev = init_v
        for tt in range(NT):
            src = xo if tt < NQ else xp
            row0 = (tt if tt < NQ else tt - NQ) * 128
            sl = tt % 2
            t_x = b.dma("Y", f"xsl{sl}", xsl[sl][:], src[row0:row0 + 128, :],
                        waits=[t_xread.get(tt - 2)])
            b.op("V", lambda v, sl=sl: v.reduce_sum(
                negm, xsl[sl][:], axis=AX.X, negate=True), waits=[t_x])
            b.op("V", lambda v: v.tensor_scalar_mul(negm, negm, 1.0 / D))
            t_sq = b.op("S", lambda s, sl=sl: s.activation(
                junk_bf[:], xsl[sl][:], AF.Square, accum_out=sq), waits=[t_x])
            b.op("V", lambda v: v.tensor_mul(m2c, negm, negm))
            b.op("V", lambda v: v.tensor_scalar(
                out=var, in0=sq, scalar1=1.0 / D, scalar2=None, op0=ALU.mult),
                waits=[t_sq])
            t_v3 = b.op("V", lambda v: v.tensor_sub(var, var, m2c))
            b.op("S", lambda s: s.activation(rstd, var, AF.Ln, bias=eps_ln),
                 waits=[t_v3])
            t_r2 = b.op("S", lambda s: s.activation(rstd, rstd, AF.Exp,
                                                    scale=-0.5))
            t_ln = b.op("V", lambda v, sl=sl: v.tensor_scalar(
                out=lnscr[:], in0=xsl[sl][:], scalar1=negm, scalar2=rstd,
                op0=ALU.add, op1=ALU.mult), waits=[t_r2])
            t_xread[tt] = t_ln
            ttr = None
            for cc in range(NCH):
                ttr = b.op("T", lambda t_, cc=cc: t_.transpose(
                    psum_bf(psA, 0, 2048)[:, cc * 128:(cc + 1) * 128],
                    lnscr[:, cc * 128:(cc + 1) * 128], ident_bf[:]),
                    waits=([t_ln, a_ev] if cc == 0 else []),
                    inc=(cc == NCH - 1))
            lnv = ln_xT[:].rearrange("p (c t) -> p c t", t=T)
            pv = psum_bf(psA, 0, 2048)[:].rearrange("p (c t) -> p c t", t=128)
            a_ev = b.op("V", lambda v, tt=tt, lnv=lnv, pv=pv: v.tensor_copy(
                lnv[:, :, tt * 128:(tt + 1) * 128], pv[:, 0:NCH, :]),
                waits=[ttr])
        bank_free(psA, 0, 1024, a_ev)
        wmA = b.wm()
        xav = xarena.rearrange("(n p) c -> n p c", p=128)
        t_pfx = None
        for blk in range(CAPTOT // 128):
            t_pfx = b.dma("Y", "pfx", xav[blk], zt[:], waits=[t_zt])
        yav = yarena.rearrange("(n p) c -> n p c", p=128)
        t_pfy = None
        for blk in range(OWN // 128):
            t_pfy = b.dma("Y", "pfy", yav[blk], zt[:, 0:D], waits=[t_zt])

        nm1 = stats[:, 5:6]
        nm2 = stats[:, 6:7]
        z1 = stats[:, 7:8]
        z2 = stats[:, 8:9]
        zd = stats[:, 9:10]
        r1s = stats[:, 10:11]
        b2s = stats[:, 11:12]
        rzd = stats[:, 12:13]
        lnv = ln_xT[:].rearrange("p (c t) -> p c t", t=T)
        t_wread = {}
        head_end = None
        for h in range(H):
            sl = h % 2
            wqs, wks, wvs = wqkv[sl]
            wq_src = Wq.rearrange("(c p) n -> p c n", p=128)[:, :, h * 256:(h + 1) * 256]
            wk_src = Wk.rearrange("(c p) n -> p c n", p=128)[:, :, h * 256:(h + 1) * 256]
            wv_src = Wv.rearrange("(c p) n -> p c n", p=128)[:, :, h * 128:(h + 1) * 128]
            war = [t_wread.get(h - 2)] + (wmA if h < 2 else [])
            half = max(1, NCH // 2)
            tw = None
            for piece in range(0, NCH, half):
                pe = min(piece + half, NCH)
                b.dma("G", f"wqk{sl}", wqs[:, piece:pe, :],
                      wq_src[:, piece:pe, :], waits=war)
                war = []
                b.dma("G", f"wqk{sl}", wks[:, piece:pe, :],
                      wk_src[:, piece:pe, :])
                tw = b.dma("G", f"wqk{sl}", wvs[:, piece:pe, :],
                           wv_src[:, piece:pe, :])
            nkb = max(1, T // 512)
            kb = T // nkb
            for comp, kT in enumerate((k1T, k2T)):
                wofs = comp * 128
                ps = psA if comp == 0 else psB
                for blk in range(nkb):
                    col = blk * kb
                    pc = (blk % 4) * 512
                    tk = mmgroup(ps, pc, kb,
                                 [wks[:, cc, wofs:wofs + 128] for cc in range(NCH)],
                                 [lnv[:, cc, col:col + kb] for cc in range(NCH)],
                                 waits=[tw, head_end] + wmA)
                    te = b.op("S", lambda s, kT=kT, col=col, ps=ps, pc=pc, kb=kb:
                              s.activation(kT[:, col:col + kb],
                                           ps[:, pc:pc + kb], AF.Copy),
                              waits=[tk])
                    bank_free(ps, pc, kb, te)
            nqb = max(1, OWN // 512)
            qb = OWN // nqb
            for comp, qT in enumerate((q1T, q2T)):
                wofs = comp * 128
                for blk in range(nqb):
                    col = blk * qb
                    pc = ((comp * nqb + blk) % 4) * 512
                    tk = mmgroup(psA, pc, qb,
                                 [wqs[:, cc, wofs:wofs + 128] for cc in range(NCH)],
                                 [lnv[:, cc, col:col + qb] for cc in range(NCH)],
                                 waits=[tw])
                    te = b.op("S", lambda s, qT=qT, col=col, pc=pc, qb=qb:
                              s.activation(qT[:, col:col + qb],
                                           psA[:, pc:pc + qb], AF.Copy,
                                           scale=c["QS"]), waits=[tk])
                    bank_free(psA, pc, qb, te)
            for ttile in range(NT):
                pc = (ttile % 4) * 512
                tk = mmgroup(psB, pc, 128,
                             [lnv[:, cc, ttile * 128:(ttile + 1) * 128]
                              for cc in range(NCH)],
                             [wvs[:, cc, :] for cc in range(NCH)],
                             waits=[tw])
                te = b.op("S", lambda s, ttile=ttile, pc=pc:
                          s.activation(v_sb[:, ttile * 128:(ttile + 1) * 128],
                                       psB[:, pc:pc + 128], AF.Copy),
                          waits=[tk])
                bank_free(psB, pc, 128, te)
            t_wread[h] = b.tok("T")
            nsb = max(1, T // 512)
            sbw = T // nsb
            pend = None
            last_diff = None
            last_pv = None
            last_ptr = None
            for q in range(NQ + 1):
                if q < NQ:
                    t_s1 = None
                    for blk in range(nsb):
                        w0 = ([bank_tok.get((id(psA), cb)) for cb in range(4)]
                              if blk == 0 else [])
                        t_s1 = b.op("T", lambda t_, blk=blk, q=q:
                                    t_.matmul(psA[:, blk * sbw:(blk + 1) * sbw],
                                              q1T[:, q * 128:(q + 1) * 128],
                                              k1T[:, blk * sbw:(blk + 1) * sbw],
                                              start=True, stop=True),
                                    waits=w0, inc=(blk == nsb - 1))
                    t_e1 = b.op("S", lambda s: s.activation(
                        e1[:], psA[:, 0:T], AF.Exp, accum_out=z1),
                        waits=[t_s1, last_diff]
                        + ([t_pfx, t_pfy] if h == 0 and q == 0 else []))
                    bank_free(psA, 0, 2048, t_e1)
                if pend is not None:
                    qp, dd = pend
                    t_pd = b.op("S", lambda s: s.activation(
                        p_sb[:, 0:T], e2[:], AF.Exp, accum_out=zd),
                        waits=[dd["t_diff"], last_ptr])
                    dd["t_pd"] = t_pd
                if q < NQ:
                    t_s2 = None
                    for blk in range(nsb):
                        w0 = ([bank_tok.get((id(psA), cb)) for cb in range(4)]
                              if blk == 0 else [])
                        t_s2 = b.op("T", lambda t_, blk=blk, q=q:
                                    t_.matmul(psA[:, blk * sbw:(blk + 1) * sbw],
                                              q2T[:, q * 128:(q + 1) * 128],
                                              k2T[:, blk * sbw:(blk + 1) * sbw],
                                              start=True, stop=True),
                                    waits=w0, inc=(blk == nsb - 1))
                if pend is not None:
                    ptv = psum_bf(psB, 0, 2048)
                    ttr = None
                    for tau in range(NT):
                        ttr = b.op("T", lambda t_, tau=tau, ptv=ptv: t_.transpose(
                            ptv[:, tau * 128:(tau + 1) * 128],
                            p_sb[:, tau * 128:(tau + 1) * 128], ident_bf[:]),
                            waits=([dd["t_pd"], bank_tok.get((id(psB), 0)),
                                    bank_tok.get((id(psB), 1))]
                                   if tau == 0 else []),
                            inc=(tau == NT - 1))
                    last_ptr = ttr
                    t_ptev = b.op("V", lambda v, ptv=ptv: v.tensor_copy(
                        pt_sb[:, 0:T], ptv[:, 0:T]), waits=[ttr, last_pv])
                    bank_free(psB, 0, 1024, t_ptev)
                    pvslot = 1024 + (qp % 2) * 512
                    t_pv = None
                    for tau in range(NT):
                        t_pv = b.op("T", lambda t_, tau=tau, pvslot=pvslot:
                                    t_.matmul(
                                        psB[:, pvslot:pvslot + 128],
                                        pt_sb[:, tau * 128:(tau + 1) * 128],
                                        v_sb[:, tau * 128:(tau + 1) * 128],
                                        start=(tau == 0), stop=(tau == NT - 1)),
                                    waits=([t_ptev,
                                            bank_tok.get((id(psB),
                                                          pvslot // 512))]
                                           if tau == 0 else []),
                                    inc=(tau == NT - 1))
                    last_pv = t_pv
                    b.op("V", lambda v: v.reciprocal(rzd, zd),
                         waits=[dd["t_pd"]])
                    t_pvev = b.op("V", lambda v, qp=qp, pvslot=pvslot, h=h:
                                  v.tensor_scalar(
                                      out=out_sb[:, qp * D + h * 128:
                                                 qp * D + (h + 1) * 128],
                                      in0=psB[:, pvslot:pvslot + 128],
                                      scalar1=rzd, scalar2=None,
                                      op0=ALU.mult), waits=[t_pv])
                    bank_free(psB, pvslot, 512, t_pvev)
                    head_end = t_pvev
                    pend = None
                if q == NQ:
                    break
                t_e2 = b.op("S", lambda s: s.activation(
                    e2[:], psA[:, 0:T], AF.Exp, accum_out=z2),
                    waits=[t_s2])
                bank_free(psA, 0, 2048, t_e2)
                b.op("V", lambda v: v.reciprocal(r1s, z1), waits=[t_e1])
                b.op("V", lambda v: v.reciprocal(b2s, z2), waits=[t_e2])
                b.op("V", lambda v, h=h: v.tensor_mul(b2s, b2s,
                                                      lam_b[:, h:h + 1]))
                b.op("V", lambda v: v.tensor_scalar_mul(e2[:], e2[:], b2s))
                t_diff = b.op("V", lambda v: v.scalar_tensor_tensor(
                    out=e2[:], in0=e1[:], scalar=r1s, in1=e2[:],
                    op0=ALU.mult, op1=ALU.subtract))
                last_diff = t_diff
                pend = (q, {"t_diff": t_diff})
        wmB = b.wm()

        tnb = None
        for j in range(NDS):
            tk = b.op("T", lambda t_, j=j: t_.matmul(
                psA[:, (j % 4) * 512:(j % 4) * 512 + DS], ones_col[:],
                ns_row[0:1, j * DS:(j + 1) * DS], start=True, stop=True),
                waits=wmB + [bank_tok.get((id(psA), j % 4))])
            tnb = b.op("V", lambda v, j=j: v.tensor_copy(
                ns_b[:, j * DS:(j + 1) * DS],
                psA[:, (j % 4) * 512:(j % 4) * 512 + DS]), waits=[tk])
            bank_free(psA, (j % 4) * 512, DS, tnb)
        sumsq = stats[:, 14:15]
        rstd2 = stats[:, 15:16]
        c_ev = None
        t_t1 = None
        for q in range(NQ):
            b.op("S", lambda s, q=q: s.activation(
                junk_bf[:], out_sb[:, q * D:(q + 1) * D], AF.Square,
                accum_out=sumsq), waits=wmB + [t_t1])
            b.op("S", lambda s: s.activation(
                rstd2, sumsq, AF.Ln, scale=1.0 / D, bias=eps_rms))
            t_b_ = b.op("S", lambda s: s.activation(rstd2, rstd2, AF.Exp,
                                                    scale=-0.5))
            b.op("V", lambda v: v.tensor_mul(rstd2, rstd2, one_minus),
                 waits=[t_b_])
            t_t1 = b.op("V", lambda v, q=q: v.tensor_scalar(
                out=tmp_f[:], in0=out_sb[:, q * D:(q + 1) * D],
                scalar1=rstd2, scalar2=None, op0=ALU.mult))
            t_on = b.op("V", lambda v, q=q: v.tensor_mul(
                out_sb[:, q * D:(q + 1) * D], tmp_f[:], ns_b[:]),
                waits=[tnb])
            ttr = None
            for cc in range(NCH):
                ttr = b.op("T", lambda t_, cc=cc, q=q: t_.transpose(
                    psum_bf(psB, 0, 2048)[:, cc * 128:(cc + 1) * 128],
                    out_sb[:, q * D + cc * 128: q * D + (cc + 1) * 128],
                    ident_bf[:]),
                    waits=([t_on, c_ev, bank_tok.get((id(psB), 0)),
                            bank_tok.get((id(psB), 1))] if cc == 0 else []),
                    inc=(cc == NCH - 1))
            onv = out_nT[:].rearrange("p (c t) -> p c t", t=OWN)
            pv = psum_bf(psB, 0, 2048)[:].rearrange("p (c t) -> p c t", t=128)
            c_ev = b.op("V", lambda v, q=q, onv=onv, pv=pv: v.tensor_copy(
                onv[:, :, q * 128:(q + 1) * 128], pv[:, 0:NCH, :]),
                waits=[ttr])
        bank_free(psB, 0, 1024, c_ev)
        wmC1 = b.wm()
        onv = out_nT[:].rearrange("p (c t) -> p c t", t=OWN)
        t_word = {}
        for dsl in range(NDS):
            sl = dsl % 2
            wo_src = Wo.rearrange("(c p) n -> p c n", p=128)[:, :, dsl * DS:(dsl + 1) * DS]
            half = max(1, NCH // 2)
            war = [t_word.get(dsl - 2)] + (wmC1 if dsl < 2 else [])
            twd = None
            for piece in range(0, NCH, half):
                pe = min(piece + half, NCH)
                twd = b.dma("G", f"wo{sl}", wosl[sl][:, piece:pe, :],
                            wo_src[:, piece:pe, :], waits=war)
                war = []
            for q in range(NQ):
                n_lin = dsl * NQ + q
                xslq = xsl2[n_lin % 2]
                t_x = b.dma("Y", f"xc{n_lin % 2}", xslq[:],
                            xo[q * 128:(q + 1) * 128, dsl * DS:(dsl + 1) * DS],
                            waits=[t_xread.get(("C", n_lin - 2))]
                            + (wmC1 if n_lin < 2 else []))
                pc = (q % 4) * 512
                tk = mmgroup(psA, pc, DS,
                             [onv[:, cc, q * 128:(q + 1) * 128]
                              for cc in range(NCH)],
                             [wosl[sl][:, cc, :] for cc in range(NCH)],
                             waits=[twd] + (wmC1 if n_lin == 0 else []))
                t_add = b.op("V", lambda v, q=q, dsl=dsl, pc=pc, xslq=xslq:
                             v.scalar_tensor_tensor(
                                 out=resid[:, q * D + dsl * DS:
                                           q * D + (dsl + 1) * DS],
                                 in0=psA[:, pc:pc + DS], scalar=1.0,
                                 in1=xslq[:], op0=ALU.mult, op1=ALU.add),
                             waits=[tk, t_x])
                bank_free(psA, pc, DS, t_add)
                t_xread[("C", n_lin)] = t_add
            t_word[dsl] = b.tok("T")
        wmC = b.wm()

        d_tr_ev = None
        for q in range(NQ):
            xs = resid[:, q * D:(q + 1) * D]
            b.op("V", lambda v, xs=xs: v.reduce_sum(
                negm, xs, axis=AX.X, negate=True), waits=wmC)
            b.op("V", lambda v: v.tensor_scalar_mul(negm, negm, 1.0 / D))
            t_sq = b.op("S", lambda s, xs=xs: s.activation(
                junk_d[:], xs, AF.Square, accum_out=sq), waits=wmC)
            b.op("V", lambda v: v.tensor_mul(m2c, negm, negm))
            b.op("V", lambda v: v.tensor_scalar(
                out=var, in0=sq, scalar1=1.0 / D, scalar2=None, op0=ALU.mult),
                waits=[t_sq])
            t_v3 = b.op("V", lambda v: v.tensor_sub(var, var, m2c))
            b.op("S", lambda s: s.activation(rstd, var, AF.Ln, bias=eps_ln),
                 waits=[t_v3])
            t_r2 = b.op("S", lambda s: s.activation(rstd, rstd, AF.Exp,
                                                    scale=-0.5))
            t_ln = b.op("V", lambda v, xs=xs: v.tensor_scalar(
                out=lnscr[:], in0=xs, scalar1=negm, scalar2=rstd,
                op0=ALU.add, op1=ALU.mult), waits=[t_r2])
            ttr = None
            for cc in range(NCH):
                ttr = b.op("T", lambda t_, cc=cc: t_.transpose(
                    psum_bf(psA, 0, 2048)[:, cc * 128:(cc + 1) * 128],
                    lnscr[:, cc * 128:(cc + 1) * 128], ident_bf[:]),
                    waits=([t_ln, d_tr_ev, bank_tok.get((id(psA), 0)),
                            bank_tok.get((id(psA), 1))] if cc == 0 else []),
                    inc=(cc == NCH - 1))
            l2v = ln2_xT[:].rearrange("p (c t) -> p c t", t=OWN)
            pv = psum_bf(psA, 0, 2048)[:].rearrange("p (c t) -> p c t", t=128)
            d_tr_ev = b.op("V", lambda v, q=q, l2v=l2v, pv=pv: v.tensor_copy(
                l2v[:, :, q * 128:(q + 1) * 128], pv[:, 0:NCH, :]),
                waits=[ttr])
        bank_free(psA, 0, 1024, d_tr_ev)
        l2v = ln2_xT[:].rearrange("p (c t) -> p c t", t=OWN)
        nm1g = stats[:, 16:17]
        t4s = stats[:, 17:18]
        z4 = stats[:, 18:19]
        rz4 = stats[:, 19:20]
        top8 = stats[:, 24:32]
        for q in range(NQ):
            tk = mmgroup(psA, 0, E,
                         [l2v[:, cc, q * 128:(q + 1) * 128]
                          for cc in range(NCH)] + [ones_col[:]],
                         [gwT[:, cc * 32:cc * 32 + E] for cc in range(NCH)]
                         + [gb_row[:]],
                         waits=[d_tr_ev])
            b.op("V", lambda v: v.max(out=top8, in_=psA[:, 0:E]), waits=[tk])
            b.op("V", lambda v: v.max_index(idx8[:], top8[:], psA[:, 0:E]))
            b.op("V", lambda v, q=q: v.tensor_copy(
                topidx[:, q * TOPK:(q + 1) * TOPK], idx8[:, 0:TOPK]))
            t_n1 = b.op("V", lambda v: v.tensor_scalar_mul(
                nm1g, top8[:, 0:1], -1.0))
            t_t4 = b.op("S", lambda s: s.activation(
                t4s, top8[:, TOPK - 1:TOPK], AF.Exp, bias=nm1g), waits=[t_n1])
            t_eL = b.op("S", lambda s: s.activation(
                eL_sb[:, 0:E], psA[:, 0:E], AF.Exp, bias=nm1g))
            b.op("V", lambda v, q=q: v.scalar_tensor_tensor(
                out=w_sb[:, q * E:(q + 1) * E], in0=eL_sb[:, 0:E], scalar=t4s,
                in1=eL_sb[:, 0:E], op0=ALU.is_ge, op1=ALU.mult,
                accum_out=z4), waits=[t_t4, t_eL])
            b.op("V", lambda v: v.reciprocal(rz4, z4))
            t_ws = b.op("V", lambda v, q=q: v.tensor_scalar_mul(
                w_sb[:, q * E:(q + 1) * E], w_sb[:, q * E:(q + 1) * E], rz4))
            bank_free(psA, 0, 512, t_eL)
            t_wt = b.op("T", lambda t_, q=q: t_.transpose(
                psA[0:E, 512:512 + 128], w_sb[:, q * E:(q + 1) * E],
                ident_f[:]), waits=[t_ws, bank_tok.get((id(psA), 1))])
            t_wtv = b.op("V", lambda v, q=q: v.tensor_copy(
                wT_sb[0:E, q * 128:(q + 1) * 128], psA[0:E, 512:512 + 128]),
                waits=[t_wt])
            bank_free(psA, 512, 512, t_wtv)
            for j in range(NDS):
                pc = 1024 + (j % 2) * 512
                tk2 = b.op("T", lambda t_, q=q, j=j, pc=pc: t_.matmul(
                    psA[:, pc:pc + DS], wT_sb[0:E, q * 128:(q + 1) * 128],
                    bd_all[0:E, j * DS:(j + 1) * DS], start=True, stop=True),
                    waits=[t_wtv, bank_tok.get((id(psA), pc // 512))])
                ta = b.op("V", lambda v, q=q, j=j, pc=pc: v.tensor_add(
                    resid[:, q * D + j * DS: q * D + (j + 1) * DS],
                    resid[:, q * D + j * DS: q * D + (j + 1) * DS],
                    psA[:, pc:pc + DS]), waits=[tk2])
                bank_free(psA, pc, DS, ta)
        t_gate_done = b.tok("V")

        if b.active == "G":
            bc_x_reg = b.nc.gpsimd.alloc_register("bc_x")
            bc_y_reg = b.nc.gpsimd.alloc_register("bc_y")
            b.nc.gpsimd.reg_mov(bc_x_reg, CAPTOT - 1)
            b.nc.gpsimd.reg_mov(bc_y_reg, OWN - 1)
        else:
            bc_x_reg = bc_y_reg = None

        sl_t = stats[:, 33:34]
        g_t = stats[:, 34:35]
        ovf_t = stats[:, 35:36]
        t_mk = b.op("V", lambda v: v.tensor_scalar(
            out=maskT[:], in0=wT_sb[0:E, 0:OWN], scalar1=0.0, scalar2=None,
            op0=ALU.is_gt), waits=[t_gate_done])
        t_sc = b.op("V", lambda v: v.tensor_tensor_scan(
            out=posT[:], data0=maskT[:], data1=maskT[:], initial=0.0,
            op0=ALU.add, op1=ALU.bypass))
        t_idx_done = None
        for tt in range(NQ):
            t_tr = b.op("T", lambda t_, tt=tt: t_.transpose(
                psA[0:128, 0:E], posT[0:E, tt * 128:(tt + 1) * 128],
                ident_f[0:E, 0:E]),
                waits=[t_sc, bank_tok.get((id(psA), 0))])
            t_ptk = b.op("V", lambda v: v.tensor_copy(eL_sb[:], psA[:, 0:E]),
                         waits=[t_tr])
            bank_free(psA, 0, 512, t_ptk)
            for j in range(TOPK):
                col = tt * TOPK + j
                ej = topidx[:, col:col + 1]
                b.op("V", lambda v, ej=ej: v.tensor_scalar(
                    out=gcl[:, 0:E], in0=iota_e[:], scalar1=ej, scalar2=None,
                    op0=ALU.is_equal))
                b.op("V", lambda v: v.tensor_mul(gcl[:, 0:E], gcl[:, 0:E],
                                                 eL_sb[:]))
                b.op("V", lambda v: v.reduce_sum(sl_t, gcl[:, 0:E], axis=AX.X))
                b.op("V", lambda v, ej=ej: v.scalar_tensor_tensor(
                    out=g_t, in0=ej, scalar=float(CAP), in1=sl_t,
                    op0=ALU.mult, op1=ALU.add))
                b.op("V", lambda v: v.tensor_scalar_add(g_t, g_t, -1.0))
                b.op("V", lambda v: v.tensor_scalar(
                    out=ovf_t, in0=sl_t, scalar1=float(CAP + 1), scalar2=None,
                    op0=ALU.is_ge))
                b.op("V", lambda v: v.scalar_tensor_tensor(
                    out=g_t, in0=ovf_t, scalar=2.0e9, in1=g_t,
                    op0=ALU.mult, op1=ALU.add))
                t_idx_done = b.op("V", lambda v, col=col: v.tensor_copy(
                    idx_sb[:, col:col + 1], g_t))

        wv_view = lnscr2[:, D:D + 2 * E].bitcast(F32)
        ti_view = lnscr2[:, D + 2 * E:D + 2 * E + 2].bitcast(mybir.dt.int32)
        b.op("V", lambda v: v.memset(lnscr2[:, D + 2 * E + 2:RW], 0.0),
             waits=wmC)
        t_sca = None
        for tt in range(NQ):
            ttr = None
            for cc in range(NCH):
                ttr = b.op("T", lambda t_, cc=cc, tt=tt: t_.transpose(
                    psum_bf(psA, 0, 2048)[:, cc * 128:(cc + 1) * 128],
                    ln2_xT[:, cc * OWN + tt * 128: cc * OWN + (tt + 1) * 128],
                    ident_bf[:]),
                    waits=([t_idx_done, bank_tok.get((id(psA), 0)),
                            bank_tok.get((id(psA), 1))] if cc == 0 else []),
                    inc=(cc == NCH - 1))
            pv = psum_bf(psA, 0, 2048)[:].rearrange("p (c t) -> p c t", t=128)
            ln2v = lnscr2[:, 0:D].rearrange("p (c t) -> p c t", t=128)
            t_ev = b.op("V", lambda v, pv=pv, ln2v=ln2v: v.tensor_copy(
                ln2v, pv[:, 0:NCH, :]),
                waits=[ttr, t_sca])
            bank_free(psA, 0, 1024, t_ev)
            b.op("V", lambda v, tt=tt: v.tensor_copy(
                wv_view, w_sb[:, tt * E:(tt + 1) * E]))
            t_row = b.op("V", lambda v, tt=tt: v.tensor_copy(
                ti_view, tokid_t[:, tt:tt + 1]))
            for j in range(TOPK):
                col = tt * TOPK + j
                t_sca = b.dma("G", "sca", None, None) if False else None
                if b.active == "G":
                    for tk_ in [t_row, t_pfx]:
                        b.wait(("G"), tk_)
                b.dmacnt["sca"] += 16
                if b.active == "G":
                    b.nc.gpsimd.indirect_dma_start(
                        out=xarena[:],
                        out_offset=bass.IndirectOffsetOnAxis(
                            ap=idx_sb[:, col:col + 1], axis=0),
                        in_=lnscr2[:],
                        in_offset=None,
                        bounds_check=bc_x_reg,
                        oob_is_err=False,
                    ).then_inc(env["dsems"]["sca"], 16)
                t_sca = (env["dsems"]["sca"], b.dmacnt["sca"])
        t_scat_all = t_sca

        gu_read = {}
        wd_read = {}

        def issue_gu(k, first_waits=()):
            e_, i_ = divmod(k, NI)
            if e_ >= E:
                return None
            sl = k % 3
            wgp, wup = gusl[sl]
            src_g = wg_h[e_].rearrange("(c p) n -> p c n", p=128)[:, :, i_ * 128:(i_ + 1) * 128]
            src_u = wu_h[e_].rearrange("(c p) n -> p c n", p=128)[:, :, i_ * 128:(i_ + 1) * 128]
            war = list(first_waits) + [gu_read.get(k - 3)]
            half = max(1, NCH // 2)
            tk = None
            for piece in range(0, NCH, half):
                pe_ = min(piece + half, NCH)
                b.dma("G", f"gu{sl}", wgp[:, piece:pe_, :],
                      src_g[:, piece:pe_, :], waits=war)
                war = []
                tk = b.dma("G", f"gu{sl}", wup[:, piece:pe_, :],
                           src_u[:, piece:pe_, :])
            return tk

        def issue_wd(k, first_waits=()):
            e_, j_ = divmod(k, NDS)
            if e_ >= E:
                return None
            sl = k % 2
            src_d = wd_h[e_].rearrange("(c p) n -> p c n", p=128)[:, :, j_ * DS:(j_ + 1) * DS]
            return b.dma("G", f"wd{sl}", wdsl[sl][:], src_d,
                         waits=list(first_waits) + [wd_read.get(k - 2)])

        gu_tok = {}
        wd_tok = {}
        for k in range(3):
            gu_tok[k] = issue_gu(k, first_waits=wmC + [t_scat_all])
        for k in range(2):
            wd_tok[k] = issue_wd(k, first_waits=wmC)

        xav2 = xarena.rearrange("(e t p) c -> e p t c", p=128, t=NCT)
        xtv = xtT[:].rearrange("p (c s) -> p c s", s=CAP)
        t_xt_done = {}
        t_gu_mm = {}
        t_y_read = {}
        ya_tok = {}
        t_wtib = {}
        ysc_n = [0]
        t_act_read = {}
        for e in range(E):
            t_xl = b.dma("Y", "xld", xt_sb[:], xav2[e],
                         waits=[t_scat_all, t_xt_done.get(e - 1)]
                         + (t_wtib.get(e - 1) or []))
            for tile in range(NCT):
                ttr = None
                for cc in range(NCH):
                    ttr = b.op("T", lambda t_, cc=cc, tile=tile: t_.transpose(
                        psum_bf(psA, 0, 2048)[:, cc * 128:(cc + 1) * 128],
                        xt_sb[:, tile, cc * 128:(cc + 1) * 128], ident_bf[:]),
                        waits=([t_xl, bank_tok.get((id(psA), 0)),
                                bank_tok.get((id(psA), 1))] if cc == 0 else []),
                        inc=(cc == NCH - 1))
                pv = psum_bf(psA, 0, 2048)[:].rearrange("p (c t) -> p c t",
                                                        t=128)
                tev = b.op("V", lambda v, pv=pv, tile=tile: v.tensor_copy(
                    xtv[:, :, tile * 128:(tile + 1) * 128], pv[:, 0:NCH, :]),
                    waits=[ttr, t_gu_mm.get(e - 1)])
                bank_free(psA, 0, 1024, tev)
            t_xt_done[e] = b.tok("T")
            for tile in range(NCT):
                b.op("V", lambda v, tile=tile: v.tensor_copy(
                    wbuf[:, tile:tile + 1],
                    xt_sb[:, tile, D + 2 * e:D + 2 * e + 2].bitcast(F32)),
                    waits=[t_xl] + (ya_tok.get(e - 1) or []))
                b.op("V", lambda v, tile=tile: v.tensor_copy(
                    tibuf[:, tile:tile + 1],
                    xt_sb[:, tile, D + 2 * E:D + 2 * E + 2].bitcast(
                        mybir.dt.int32)))
            t_wtib[e] = [b.tok("V")]
            t_xtT_ready = b.tok("V")
            for i in range(NI):
                k = e * NI + i
                sl = k % 3
                wgp, wup = gusl[sl]
                par = i % 2
                colg, colu = (0, 512) if par == 0 else (1024, 1536)
                tg_ = mmgroup(psA, colg, CAP if CAP <= 512 else 512,
                              [wgp[:, cc, :] for cc in range(NCH)],
                              [xtv[:, cc, 0:CAP] for cc in range(NCH)],
                              waits=[gu_tok.get(k), t_xtT_ready])
                tu_ = mmgroup(psA, colu, CAP,
                              [wup[:, cc, :] for cc in range(NCH)],
                              [xtv[:, cc, 0:CAP] for cc in range(NCH)])
                bcg = bgT_all[:, i * E + e:i * E + e + 1]
                bcu = buT_all[:, i * E + e:i * E + e + 1]
                t1 = b.op("V", lambda v, colg=colg, bcg=bcg: v.tensor_scalar(
                    out=gcl[:], in0=psA[:, colg:colg + CAP], scalar1=bcg,
                    scalar2=7.0, op0=ALU.add, op1=ALU.min), waits=[tg_])
                t2 = b.op("S", lambda s: s.activation(
                    sig[:], gcl[:], AF.Sigmoid, scale=1.702), waits=[t1])
                b.op("V", lambda v, colu=colu, bcu=bcu: v.tensor_scalar(
                    out=ucl[:], in0=psA[:, colu:colu + CAP], scalar1=bcu,
                    scalar2=7.0, op0=ALU.add, op1=ALU.min), waits=[tu_])
                t4_ = b.op("V", lambda v: v.tensor_scalar_max(
                    ucl[:], ucl[:], -7.0))
                t5 = b.op("V", lambda v: v.tensor_mul(gcl[:], gcl[:], sig[:]),
                          waits=[t2])
                b.op("V", lambda v, i=i: v.scalar_tensor_tensor(
                    out=actT[:, i * CAP:(i + 1) * CAP],
                    in0=ucl[:], scalar=1.0, in1=gcl[:],
                    op0=ALU.add, op1=ALU.mult),
                    waits=[t_act_read.get(e - 1)])
                bank_free(psA, colg, CAP, t5)
                bank_free(psA, colu, CAP, t4_)
                gu_read[k] = b.tok("T")
                gu_tok[k + 3] = issue_gu(k + 3)
            t_gu_mm[e] = b.tok("T")
            t_act_ready = b.tok("V")
            yr_toks = []
            t_ev_tile = [None] * NCT
            for j in range(NDS):
                k = e * NDS + j
                tk_d = wd_tok.get(k)
                sl = k % 2
                for tile in range(NCT):
                    pc = (tile % 4) * 512
                    tk = mmgroup(psB, pc, DS,
                                 [actT[:, i2 * CAP + tile * 128:
                                       i2 * CAP + (tile + 1) * 128]
                                  for i2 in range(NI)],
                                 [wdsl[sl][:, i2, :] for i2 in range(NI)],
                                 waits=[tk_d, t_act_ready])
                    yb = ysbt[tile]
                    wcol = wbuf[:, tile:tile + 1]
                    par = tile % 2
                    tev = b.op("V", lambda v, pc=pc, yb=yb, wcol=wcol, j=j:
                               v.tensor_scalar(
                                   out=yb[:, j * DS:(j + 1) * DS],
                                   in0=psB[:, pc:pc + DS],
                                   scalar1=wcol, scalar2=None, op0=ALU.mult),
                               waits=[tk]
                               + ([(env["dsems"][f"ya{par}"],
                                    b.dmacnt[f"ya{par}"])] if j == 0 else []))
                    bank_free(psB, pc, DS, tev)
                    yr_toks.append(tev)
                    t_ev_tile[tile] = tev
                wd_read[k] = b.tok("T")
                wd_tok[k + 2] = issue_wd(k + 2)
            for tile in range(NCT):
                ti_v = tibuf[:, tile:tile + 1]
                par = tile % 2
                if b.active == "G":
                    b.wait("G", t_ev_tile[tile])
                    b.wait("G", t_pfy)
                    b.wait("G", (env["dsems"]["ya0"], b.dmacnt["ya0"]))
                    b.wait("G", (env["dsems"]["ya1"], b.dmacnt["ya1"]))
                b.dmacnt[f"ya{par}"] += 16
                if b.active == "G":
                    b.nc.gpsimd.indirect_dma_start(
                        out=yarena,
                        out_offset=bass.IndirectOffsetOnAxis(
                            ap=ti_v, axis=0),
                        in_=ysbt[tile][:],
                        in_offset=None,
                        bounds_check=bc_y_reg,
                        oob_is_err=False,
                        compute_op=ALU.add,
                    ).then_inc(env["dsems"][f"ya{par}"], 16)
            t_act_read[e] = b.tok("T")
            t_y_read[e] = yr_toks[-2:]
            ya_tok[e] = [(env["dsems"]["ya0"], b.dmacnt["ya0"]),
                         (env["dsems"]["ya1"], b.dmacnt["ya1"])]
        t_ya_all = [(env["dsems"]["ya0"], b.dmacnt["ya0"]),
                    (env["dsems"]["ya1"], b.dmacnt["ya1"])]

        t_cv = None
        for tt in range(NQ):
            t_yl = b.dma("Y", "yrd", ysbt[0][:], yav[tt],
                         waits=t_ya_all + [t_cv])
            t_cv = b.op("V", lambda v: v.tensor_copy(tmp_f[:], ysbt[0][:]),
                        waits=[t_yl])
            b.op("V", lambda v, tt=tt: v.tensor_add(
                resid[:, tt * D:(tt + 1) * D], resid[:, tt * D:(tt + 1) * D],
                tmp_f[:]))

        wmE = b.wm()
        st_tok = None
        for q in range(NQ):
            st_tok = b.dma("Y", "st", y_h[q * 128:(q + 1) * 128, :],
                           resid[:, q * D:(q + 1) * D], waits=wmE)
        if b.active == "Y":
            b.nc.sync.wait_ge(env["dsems"]["st"], st_tok[1])

    with es_glob:
        with nc.Block() as blk:
            @blk.tensor
            def _(t):
                walk(EB(nc, "T", env))

            @blk.vector
            def _(v):
                walk(EB(nc, "V", env))

            @blk.scalar
            def _(s):
                walk(EB(nc, "S", env))

            @blk.gpsimd
            def _(g):
                walk(EB(nc, "G", env))

            @blk.sync
            def _(y):
                walk(EB(nc, "Y", env))
    return nc


def kernel(L0_attn_x, L0_attn_Wq, L0_attn_Wk, L0_attn_Wv, L0_attn_Wo,
           L0_attn_norm_scale, L0_attn_lam, L0_ffn_gate_w, L0_ffn_gate_b,
           L0_ffn_w_gate, L0_ffn_b_gate, L0_ffn_w_up, L0_ffn_b_up,
           L0_ffn_w_down, L0_ffn_b_down):
    x = np.ascontiguousarray(np.asarray(L0_attn_x, dtype=np.float32))
    B, S, Dm = x.shape
    nc = build_program(CFG_FULL)
    f32c = lambda a: np.ascontiguousarray(np.asarray(a, dtype=np.float32))
    shared = {
        "Wq": f32c(L0_attn_Wq), "Wk": f32c(L0_attn_Wk), "Wv": f32c(L0_attn_Wv),
        "Wo": f32c(L0_attn_Wo), "norm_scale": f32c(L0_attn_norm_scale),
        "lam": f32c(L0_attn_lam), "gate_w": f32c(L0_ffn_gate_w),
        "gate_b": f32c(L0_ffn_gate_b), "w_gate": f32c(L0_ffn_w_gate),
        "b_gate": f32c(L0_ffn_b_gate), "w_up": f32c(L0_ffn_w_up),
        "b_up": f32c(L0_ffn_b_up), "w_down": f32c(L0_ffn_w_down),
        "b_down": f32c(L0_ffn_b_down),
    }
    in_maps = []
    half = S // 2
    for core in range(8):
        bidx, h = core // 2, core % 2
        m = dict(shared)
        m["x_own"] = np.ascontiguousarray(x[bidx, h * half:(h + 1) * half])
        m["x_part"] = np.ascontiguousarray(x[bidx, (1 - h) * half:(2 - h) * half])
        in_maps.append(m)
    res = run_bass_kernel_spmd(nc, in_maps, core_ids=list(range(8)))
    y = np.empty((B, S, Dm), dtype=np.float32)
    for core in range(8):
        bidx, h = core // 2, core % 2
        y[bidx, h * half:(h + 1) * half] = res.results[core]["y_own"]
    return y
